# version 15
# baseline (speedup 1.0000x reference)
import numpy as np
from contextlib import ExitStack

import concourse.bass as bass
import concourse.mybir as mybir
from concourse.bass_utils import run_bass_kernel_spmd

F32 = mybir.dt.float32
BF16 = mybir.dt.bfloat16
FP16 = mybir.dt.float16
AF = mybir.ActivationFunctionType
ALU = mybir.AluOpType
AX = mybir.AxisListType

D = 2048
DC = 16
NT = 2048
NB = 4
NH = 16
NKV = 4
HD = 128
NE = 12
CAP = 1707
SCALE = 1.0 / float(np.sqrt(128.0))
EXPB = -20.0

BLOCKS = [[0, 3, 4, 7], [1, 2, 5, 6]]


def _tile_seq():
    out = []
    for c in range(2):
        for b in BLOCKS[c]:
            for q in range(4):
                out.append((b * 512 + q * 128, b * 512 + q * 128 + 128))
    return out


TILE_SEQ = _tile_seq()
TILES = []
for g in range(2):
    row = []
    for h in range(2):
        sbs = [BLOCKS[c][2 * g + h] for c in range(2)]
        lst = []
        for t, (lo, hi) in enumerate(TILE_SEQ):
            if any(lo < (sb + 1) * 512 for sb in sbs):
                need_mask = any(hi > sb * 512 + 1 for sb in sbs)
                lst.append((t, need_mask))
        row.append(lst)
    TILES.append(row)

SEMS = ("pe", "act", "dve",
        "sc", "ql0", "ql1", "kv0", "kv1", "ow0", "ow1", "qw0", "qw1",
        "kw0", "kw1", "vw", "obl", "ybl", "yw0", "yw1", "xl0", "xl1",
        "xw0", "xw1", "cw", "gl",
        "gc", "xb0", "xb1", "ws0", "ws1", "wo0", "wo1",
        "w130", "w131", "w132", "w133", "w2l", "cs", "yb2", "pla",
        "cka", "cva", "ckb", "cvb", "cf1", "cf2")


class Prog:
    ENGINES = ("sync", "gpsimd", "tensor", "scalar", "vector")

    def __init__(self):
        self.items = {e: [] for e in self.ENGINES}
        self.counts = {}
        self.waited = {}

    def sem(self, name):
        self.counts[name] = 0

    def now(self, s):
        return self.counts[s]

    AUTOSEM = {"tensor": "pe", "scalar": "act", "vector": "dve"}
    POOL_OPS = ("tensor_tensor", "tensor_copy", "tensor_scalar")

    def op(self, eng, method, *args, waits=(), inc=None, inc_by=1, **kw):
        if inc is None and method != "nop":
            inc = self.AUTOSEM.get(eng)
            if eng == "gpsimd" and method in self.POOL_OPS:
                inc = "pla"
        waits = list(waits)
        if eng in ("vector", "scalar") and method != "nop":
            es = self.AUTOSEM[eng]
            waits.append((es, self.counts[es]))
        elif eng == "gpsimd" and method in self.POOL_OPS:
            waits.append(("pla", self.counts["pla"]))
        ws = []
        seen = self.waited.setdefault(eng, {})
        for s, v in waits:
            if v <= 0 or v <= seen.get(s, 0):
                continue
            assert v <= self.counts[s], (eng, method, s, v, self.counts[s])
            seen[s] = v
            ws.append((s, v))
        if inc is not None:
            self.counts[inc] += inc_by
        self.items[eng].append((method, args, kw, ws,
                                (inc, inc_by) if inc else None))
        return self.counts[inc] if inc is not None else None

    def dma(self, eng, sem, out, in_, waits=(), **kw):
        return self.op(eng, "dma_start", out=out, in_=in_, waits=waits,
                       inc=sem, inc_by=16, **kw)

    def full(self, *names):
        return [(s, self.counts[s]) for s in names]


def _replay(prog, sems, engname):
    def run(eng):
        for method, args, kw, ws, inc in prog.items[engname]:
            for s, v in ws:
                eng.wait_ge(sems[s], v)
            inst = getattr(eng, method)(*args, **kw)
            if inc is not None:
                inst.then_inc(sems[inc[0]], inc[1])
    return run


def build_nc():
    nc = bass.Bass("TRN2", target_bir_lowering=False, debug=False)

    def din(name, shape, dt=F32):
        return nc.dram_tensor(name, shape, dt, kind="ExternalInput").ap()

    x_fm = din("x_fm", [D, NT])
    waq = din("waq", [NH, 128, D])
    wak = din("wak", [NKV, 128, D])
    wav = din("wav", [2, 128, 4096])
    wpa = din("wpa", [DC, 128, D])
    nwa = din("nwa", [128, DC])
    wbq = din("wbq", [NH, 128, D])
    wbk = din("wbk", [NKV, 128, D])
    wbv = din("wbv", [2, 128, 4096])
    wbo = din("wbo", [DC, 128, D])
    gw = din("gw", [128, DC * NE])
    w13 = din("w13", [NE, 128, 8192])
    w2p = din("w2p", [16, 128, 3072])
    CSa = din("csa", [2, 128, NT])
    CSb = din("csb", [2, 128, NT])
    prm = din("prm", [2, 128, 128])
    qidc = din("qidc", [128, NT])
    kidc = din("kidc", [128, 32])
    idn = din("idn", [128, 128])
    onescol = din("onescol", [128, 1])
    onesrow = din("onesrow", [1, 128])
    oh12 = din("oh12", [12, 12])
    prec = din("prec", [64, 8])
    cbias = din("cbias", [128, 4])

    y_fm = nc.dram_tensor("y_fm", [D, NT], F32, kind="ExternalOutput").ap()

    Xd = nc.dram_tensor("Xd", [D, NT], F32).ap()
    Qd = nc.dram_tensor("Qd", [D, NT], BF16).ap()
    Od = nc.dram_tensor("Od", [D, NT], BF16).ap()
    Yd = nc.dram_tensor("Yd", [D, NT], BF16).ap()
    kown = nc.dram_tensor("kown", [512, NT], BF16)
    vown = nc.dram_tensor("vown", [512, NT], BF16)
    kg = nc.dram_tensor("kg", [1024, NT], BF16)
    vg = nc.dram_tensor("vg", [1024, NT], BF16)
    contrib = nc.dram_tensor("contrib", [1, 96], F32)
    gath = nc.dram_tensor("gath", [8, 96], F32)

    p = Prog()
    for s in SEMS:
        p.sem(s)

    st_ctx = ExitStack()
    sb = lambda n, sh, dt: st_ctx.enter_context(nc.sbuf_tensor(n, sh, dt))

    xln = sb("xln", [128, DC * NT], BF16)
    PB = sb("PB", [128, 24576], BF16)
    PF = sb("PF", [128, 3072], F32)
    GS = sb("GS", [128, 24576], BF16)
    ROWS = sb("ROWS", [1, 4096], F32)
    RT1 = sb("RT1", [12, NT], FP16)
    RT2 = sb("RT2", [12, NT], FP16)
    OS1 = sb("OS1", [128, 192], BF16)
    OS2 = sb("OS2", [128, 192], BF16)
    id_bf = sb("id_bf", [128, 128], BF16)
    RWS = sb("RWS", [128, 32], F32)
    rs2_sb = sb("rs2_sb", [12, 512], F32)
    id_sb = sb("id_sb", [128, 128], F32)
    prm_sb = sb("prm_sb", [128, 256], BF16)
    onesc = sb("onesc", [128, 1], BF16)
    onesc32 = sb("onesc32", [128, 1], F32)
    onesr = sb("onesr", [1, 128], F32)
    oh_sb = sb("oh_sb", [12, 12], F32)
    prec_sb = sb("prec_sb", [64, 8], F32)
    gw_sb = sb("gw_sb", [128, DC * NE], BF16)
    nwa_sb = sb("nwa_sb", [128, DC], F32)
    cb_sb = sb("cb_sb", [128, 4], F32)
    kid_sb = sb("kid_sb", [128, 32], F32)
    g64_sb = sb("g64_sb", [64, 12], F32)
    cnts_sb = sb("cnts_sb", [12, 8], F32)
    thr_sb = sb("thr_sb", [12, 8], F32)
    rs_sb = sb("rs_sb", [12, 512], F32)
    sel = sb("sel", [128, 96], F32)
    rw_sb = sb("rw_sb", [128, 8], F32)
    psA = st_ctx.enter_context(nc.psum_tensor("psA", [128, 4096], F32))

    def pb(o, n):
        return PB[:, o:o + n]

    def pf(o, n):
        return PF[:, o:o + n]

    qidc_sb = pf(0, 2048)

    for t, src in ((qidc_sb, qidc), (id_sb[:], idn), (onesr[:], onesrow),
                   (oh_sb[:], oh12), (prec_sb[:], prec), (nwa_sb[:], nwa),
                   (cb_sb[:], cbias), (kid_sb[:], kidc),
                   (onesc32[:], onescol)):
        p.dma("sync", "sc", t, src)
    for t, src in ((prm_sb[:, 0:128], prm[0]), (prm_sb[:, 128:256], prm[1]),
                   (onesc[:], onescol), (gw_sb[:], gw), (id_bf[:], idn)):
        p.dma("gpsimd", "gc", t, src)
    CONST_S = p.full("sc")
    CONST_G = p.full("gc")

    state = {("xw", 0): 0, ("xw", 1): 0}

    def emit_ln(xsrc, gate):
        xb = [pb(0, 2048), pb(2048, 2048)]
        xq = [pb(4096, 2048), pb(6144, 2048)]
        xb_pe = [0, 0]
        xb_act = [0, 0]
        xq_pe = [0, 0]
        for c in range(DC):
            i = c % 2
            w = [("pe", xb_pe[i]), ("act", xb_act[i])]
            if c == 0:
                w += gate
            ld = p.dma("gpsimd", f"xb{i}", xb[i],
                       xsrc[c * 128:(c + 1) * 128, :], waits=w)
            xb_act[i] = p.op("scalar", "activation", xq[i], xb[i], AF.Square,
                             waits=[(f"xb{i}", ld), ("pe", xq_pe[i])],
                             inc="act")
            for jj in range(4):
                xb_pe[i] = p.op(
                    "tensor", "matmul", psA[0:1, jj * 512:jj * 512 + 512],
                    onesc[:], xb[i][:, jj * 512:jj * 512 + 512],
                    start=(c == 0), stop=(c == DC - 1), skip_group_check=True,
                    waits=[(f"xb{i}", ld)] + CONST_G if jj == 0 else [],
                    inc="pe" if jj == 3 else None)
            for jj in range(4):
                xq_pe[i] = p.op(
                    "tensor", "matmul",
                    psA[0:1, 2048 + jj * 512:2048 + jj * 512 + 512],
                    onesc[:], xq[i][:, jj * 512:jj * 512 + 512],
                    start=(c == 0), stop=(c == DC - 1), skip_group_check=True,
                    waits=[("act", xb_act[i])] if jj == 0 else [],
                    inc="pe" if jj == 3 else None)
        stats_pe = p.now("pe")
        mrow = ROWS[0:1, 0:2048]
        vrow = ROWS[0:1, 2048:4096]
        p.op("vector", "tensor_scalar", mrow, psA[0:1, 0:2048], 1.0 / 2048.0,
             None, ALU.mult, waits=[("pe", stats_pe)])
        p.op("vector", "tensor_tensor", vrow, mrow, mrow, ALU.mult)
        rows_dve = p.op("vector", "scalar_tensor_tensor", out=vrow,
                        in0=psA[0:1, 2048:4096], scalar=1.0 / 2048.0,
                        in1=vrow, op0=ALU.mult, op1=ALU.subtract, inc="dve")
        p.op("scalar", "activation", vrow, vrow, AF.Ln, bias=cb_sb[0:1, 1:2],
             waits=[("dve", rows_dve)] + CONST_S)
        rstd_act = p.op("scalar", "activation", vrow, vrow, AF.Exp,
                        scale=-0.5, inc="act")
        for jj in range(4):
            p.op("tensor", "matmul", psA[:, jj * 512:jj * 512 + 512],
                 onesr[:], mrow[0:1, jj * 512:jj * 512 + 512], start=True,
                 stop=True, skip_group_check=True,
                 waits=[("dve", rows_dve)] + CONST_S if jj == 0 else [])
            bc_pe = p.op("tensor", "matmul",
                         psA[:, 2048 + jj * 512:2048 + jj * 512 + 512],
                         onesr[:], vrow[0:1, jj * 512:jj * 512 + 512],
                         start=True, stop=True, skip_group_check=True,
                         waits=[("act", rstd_act)] if jj == 0 else [],
                         inc="pe" if jj == 3 else None)
        m_b = pb(8192, 2048)
        r_b = pb(10240, 2048)
        p.op("vector", "tensor_copy", m_b, psA[:, 0:2048],
             waits=[("pe", bc_pe)])
        bc_dve = p.op("vector", "tensor_copy", r_b, psA[:, 2048:4096],
                      inc="dve")
        tmp = pb(4096, 2048)
        ap_dve = [0, 0]
        last = bc_dve
        for c in range(DC):
            i = c % 2
            ld = p.dma("gpsimd", f"xb{i}", xb[i],
                       xsrc[c * 128:(c + 1) * 128, :],
                       waits=[("pe", xb_pe[i]), ("act", xb_act[i]),
                              ("dve", ap_dve[i])])
            p.op("vector", "tensor_tensor", tmp, xb[i], m_b, ALU.subtract,
                 waits=[(f"xb{i}", ld), ("dve", bc_dve)])
            last = p.op("vector", "tensor_tensor",
                        xln[:, c * 2048:(c + 1) * 2048], tmp, r_b, ALU.mult,
                        inc="dve")
            ap_dve[i] = last
            xb_pe[i] = 0
            xb_act[i] = 0
        return last

    wsl = [pb(12288, 2048), pb(14336, 2048)]
    wsl_rd = [0, 0]

    def proj_head(fidx, wsrc, xln_dve, epilog):
        i = fidx % 2
        lw = p.dma("gpsimd", f"ws{i}", wsl[i], wsrc,
                   waits=[("pe", wsl_rd[i])])
        for blk in range(NB):
            sl = blk % 2
            pp = psA[:, sl * 512:sl * 512 + 512]
            for c in range(DC):
                w = []
                if c == 0:
                    w = [(f"ws{i}", lw), ("dve", xln_dve),
                         ("act", state.get(("ppfree", sl), 0))]
                pe_c = p.op(
                    "tensor", "matmul", pp,
                    wsl[i][:, c * 128:c * 128 + 128],
                    xln[:, c * 2048 + blk * 512:c * 2048 + blk * 512 + 512],
                    start=(c == 0), stop=(c == DC - 1), skip_group_check=True,
                    waits=w, inc="pe" if c == DC - 1 else None)
            if blk >= NB - 2:
                wsl_rd[i] = pe_c
            epilog(blk, pp, pe_c)

    RP = dict(tq_w=[[], []], qo_w=[[], []], pr_w=[], cs_n=0)

    def rope_epilog(layer, dst_of_blk, wstream):
        CS = pb(16384, 4096)
        tq = [pb(20480, 512), pb(20992, 512)]
        qo = [pb(21504, 512), pb(22016, 512)]
        t1 = pb(22528, 512)
        t2 = pb(23040, 512)
        prmv = prm_sb[:, layer * 128:layer * 128 + 128]

        def ep(blk, pp, pe_c):
            i = blk % 2
            ev = p.op("scalar", "copy", tq[i], pp,
                      waits=[("pe", pe_c)] + RP["tq_w"][i], inc="act")
            state[("ppfree", blk % 2)] = ev
            sw = p.op("tensor", "matmul", psA[:, 1024:1536], prmv, tq[i],
                      start=True, stop=True, skip_group_check=True,
                      waits=[("act", ev)] + CONST_G + RP["pr_w"], inc="pe")
            t1c = p.op("vector", "tensor_tensor", t1, tq[i],
                       CS[:, blk * 512:blk * 512 + 512], ALU.mult,
                       waits=[("act", ev), ("cs", RP["cs_n"])], inc="dve")
            t2c = p.op("vector", "tensor_tensor", t2, psA[:, 1024:1536],
                       CS[:, 2048 + blk * 512:2048 + blk * 512 + 512],
                       ALU.mult, waits=[("pe", sw)], inc="dve")
            RP["pr_w"] = [("dve", t2c)]
            qoc = p.op("vector", "tensor_tensor", qo[i], t1, t2, ALU.add,
                       waits=RP["qo_w"][i], inc="dve")
            RP["tq_w"][i] = [("pe", sw), ("dve", t1c)]
            wn = p.dma("sync", f"{wstream}{i}", dst_of_blk(blk), qo[i],
                       waits=[("dve", qoc)])
            RP["qo_w"][i] = [(f"{wstream}{i}", wn)]
        return ep

    def emit_vproj(wv_src, xln_dve):
        vo_sb = pb(4096, 8192)
        pv_dve = [0, 0]
        for u in range(2):
            a, b_ = u % 2, 1 - (u % 2)
            la = p.dma("gpsimd", f"ws{a}", wsl[a], wv_src[u][:, 0:2048],
                       waits=[("pe", wsl_rd[a])])
            lb = p.dma("gpsimd", f"ws{b_}", wsl[b_], wv_src[u][:, 2048:4096],
                       waits=[("pe", wsl_rd[b_])])
            for tt in range(16):
                sl = tt % 2
                pv = psA[:, 1536 + sl * 512:1536 + sl * 512 + 256]
                for c in range(DC):
                    w = [(f"ws{a}", la), (f"ws{b_}", lb), ("dve", xln_dve),
                         ("dve", pv_dve[sl])] if c == 0 else []
                    slab = wsl[a] if c < 8 else wsl[b_]
                    cc = c % 8
                    pe_c = p.op(
                        "tensor", "matmul", pv,
                        xln[:, c * 2048 + tt * 128:
                            c * 2048 + tt * 128 + 128],
                        slab[:, cc * 256:cc * 256 + 256],
                        start=(c == 0), stop=(c == DC - 1),
                        skip_group_check=True, waits=w,
                        inc="pe" if c == DC - 1 else None)
                if tt >= 14:
                    wsl_rd[a] = pe_c
                    wsl_rd[b_] = pe_c
                d = 0
                for jj in range(2):
                    j = 2 * u + jj
                    d = p.op("vector", "tensor_copy",
                             vo_sb[:, j * 2048 + tt * 128:
                                   j * 2048 + tt * 128 + 128],
                             pv[:, jj * 128:jj * 128 + 128],
                             waits=[("pe", pe_c)] if jj == 0 else [],
                             inc="dve" if jj == 1 else None)
                pv_dve[sl] = d
        dv = p.now("dve")
        for j in range(4):
            p.dma("sync", "vw", vown.ap()[j * 128:(j + 1) * 128, :],
                  vo_sb[:, j * 2048:(j + 1) * 2048],
                  waits=[("dve", dv)] if j == 0 else [])
        return p.full("vw")

    def emit_core(layer):
        ck, cv = ("cka", "cva") if layer == 0 else ("ckb", "cvb")
        Kb = [pb(0, 4096), pb(8192, 4096)]
        Vb = [pb(4096, 4096), pb(12288, 4096)]
        qb = [pb(16384, 1024), pb(17408, 1024)]
        eb = [pb(18432, 1024), pb(19456, 1024)]
        ob = [pb(20480, 1024), pb(21504, 1024)]
        ou = [pb(22528, 1024), pb(23552, 1024)]
        ou_rd = [0, 0]
        rb = pf(2048, 1024)
        st = [psA[:, 0:1024], psA[:, 1024:2048]]
        av = psA[:, 2048:3072]
        sm = psA[0:1, 3072:4096]
        kv_n = [0, 0]
        kv_rd = [0, 0]
        q_n = [0, 0]
        q_rd = [0, 0]
        ow_n = [0, 0]
        exp_done = [0, 0]
        av_rd = [0, 0]
        acc_n = [0, 0]
        sm_rd = [0]
        if layer == 0:
            accs = [eb[0][:, 512:1024], eb[1][:, 512:1024]]
        else:
            acc = PF[:, 0:1024]
        ki = qi = 0
        rbev = 0
        av_free = 0
        sm_free = 0
        qd_done = state["qd_done"]

        def load_kv(j, slot, first=False):
            w = [("pe", kv_rd[slot]), (ck, p.now(ck)), (cv, p.now(cv))]
            if first:
                w += qd_done
            p.dma("sync", f"kv{slot}", Kb[slot][:, 0:2048],
                  kg.ap()[j * 128:(j + 1) * 128, :], waits=w)
            p.dma("sync", f"kv{slot}", Kb[slot][:, 2048:4096],
                  kg.ap()[512 + j * 128:512 + j * 128 + 128, :])
            p.dma("sync", f"kv{slot}", Vb[slot][:, 0:2048],
                  vg.ap()[j * 128:(j + 1) * 128, :])
            p.dma("sync", f"kv{slot}", Vb[slot][:, 2048:4096],
                  vg.ap()[512 + j * 128:512 + j * 128 + 128, :])
            return p.now(f"kv{slot}")

        kv_n[0] = load_kv(0, 0, first=True)
        for g in range(2):
            for j in range(NKV):
                slot = (g * NKV + j) % 2
                for qh in range(4):
                    h = j * 4 + qh
                    qsl = qi % 2
                    q_n[qsl] = p.dma(
                        "sync", f"ql{qsl}", qb[qsl],
                        Qd[h * 128:(h + 1) * 128, g * 1024:g * 1024 + 1024],
                        waits=[("pe", q_rd[qsl])] + qd_done)
                    if qh == 0 and (g, j) != (1, NKV - 1):
                        nj, ng = (j + 1, g) if j < NKV - 1 else (0, g + 1)
                        kv_n[1 - slot] = load_kv(nj, 1 - slot)
                    if layer == 0:
                        tiles = []
                        for h01 in range(2):
                            n = len(TILES[g][h01])
                            for idx, (t, msk) in enumerate(TILES[g][h01]):
                                tiles.append((h01, t, msk, idx == n - 1))
                    else:
                        tiles = [(None, t, False, t == 31)
                                 for t in range(32)]
                    pend = None
                    first = {0: True, 1: True, None: True}

                    def flush(pd):
                        nonlocal av_free, sm_free
                        sp, h01, t, gate, fa, la = pd
                        w = [gate]
                        if fa:
                            w += [("act", av_free), ("act", sm_free)]
                        if layer == 0:
                            off = h01 * 512
                            av_rd[sp] = p.op(
                                "tensor", "matmul", av[:, off:off + 512],
                                Vb[slot][:, t * 128:t * 128 + 128],
                                eb[sp][:, 0:512], start=fa, stop=la,
                                skip_group_check=True, waits=w, inc="pe")
                            if fa:
                                acc_n[sp] = p.op(
                                    "gpsimd", "tensor_copy", accs[h01],
                                    eb[sp][:, 0:512],
                                    waits=[gate, ("pe", sm_rd[0])])
                            else:
                                acc_n[sp] = p.op(
                                    "gpsimd", "tensor_tensor", accs[h01],
                                    accs[h01], eb[sp][:, 0:512], ALU.add,
                                    waits=[gate])
                        else:
                            for jj in range(2):
                                av_rd[sp] = p.op(
                                    "tensor", "matmul",
                                    av[:, jj * 512:jj * 512 + 512],
                                    Vb[slot][:, t * 128:t * 128 + 128],
                                    eb[sp][:, jj * 512:jj * 512 + 512],
                                    start=fa, stop=la, skip_group_check=True,
                                    waits=w if jj == 0 else [],
                                    inc="pe" if jj == 1 else None)
                            if fa:
                                acc_n[sp] = p.op(
                                    "gpsimd", "tensor_copy", acc,
                                    eb[sp][:, 0:1024],
                                    waits=[gate, ("pe", sm_rd[0])])
                            else:
                                acc_n[sp] = p.op(
                                    "gpsimd", "tensor_tensor", acc, acc,
                                    eb[sp][:, 0:1024], ALU.add,
                                    waits=[gate])

                    for (h01, t, msk, is_last) in tiles:
                        sp = ki % 2
                        ww = [("act", exp_done[sp]), (f"ql{qsl}", q_n[qsl]),
                              (f"kv{slot}", kv_n[slot]), ("act", rbev)]
                        if layer == 0:
                            qk = p.op("tensor", "matmul", st[sp][:, 0:512],
                                      Kb[slot][:, t * 128:t * 128 + 128],
                                      qb[qsl][:, h01 * 512:h01 * 512 + 512],
                                      start=True, stop=True,
                                      skip_group_check=True, waits=ww,
                                      inc="pe")
                            width = 512
                        else:
                            p.op("tensor", "matmul", st[sp][:, 0:512],
                                 Kb[slot][:, t * 128:t * 128 + 128],
                                 qb[qsl][:, 0:512], start=True, stop=True,
                                 skip_group_check=True, waits=ww)
                            qk = p.op("tensor", "matmul", st[sp][:, 512:1024],
                                      Kb[slot][:, t * 128:t * 128 + 128],
                                      qb[qsl][:, 512:1024], start=True,
                                      stop=True, skip_group_check=True,
                                      inc="pe")
                            width = 1024
                        if pend is not None:
                            flush(pend)
                        ex = p.op("scalar", "activation", eb[sp][:, 0:width],
                                  st[sp][:, 0:width], AF.Exp,
                                  bias=cb_sb[:, 0:1], scale=SCALE,
                                  waits=[("pe", qk), ("pe", av_rd[sp]),
                                         ("pla", acc_n[sp])]
                                  + CONST_S, inc="act")
                        exp_done[sp] = ex
                        gate = ("act", ex)
                        if layer == 0 and msk:
                            blkcol = (2 * g + h01) * 512
                            mk = p.op("vector", "scalar_tensor_tensor",
                                      out=eb[sp][:, 0:512],
                                      in0=qidc_sb[:, blkcol:blkcol + 512],
                                      scalar=kid_sb[:, t:t + 1],
                                      in1=eb[sp][:, 0:512],
                                      op0=ALU.is_ge, op1=ALU.mult,
                                      waits=[("act", ex)], inc="dve")
                            gate = ("dve", mk)
                        pend = (sp, h01, t, gate, first[h01], is_last)
                        first[h01] = False
                        ki += 1
                    flush(pend)
                    accq = p.now("pla")
                    for jj in range(2):
                        smw = [("pla", accq), ("act", sm_free)] \
                            if jj == 0 else []
                        if layer == 0:
                            smsrc, smlhs = accs[jj], onesc[:]
                        else:
                            smsrc = acc[:, jj * 512:jj * 512 + 512]
                            smlhs = onesc32[:]
                        p.op("tensor", "matmul",
                             sm[0:1, jj * 512:jj * 512 + 512], smlhs, smsrc,
                             start=True, stop=True, skip_group_check=True,
                             waits=smw, inc="pe" if jj == 1 else None)
                    last_pe = p.now("pe")
                    sm_rd[0] = last_pe
                    q_rd[qsl] = last_pe
                    kv_rd[slot] = last_pe
                    p.op("scalar", "activation", ROWS[0:1, 0:1024], sm,
                         AF.Ln, waits=[("pe", last_pe)])
                    recip = p.op("scalar", "activation", ROWS[0:1, 0:1024],
                                 ROWS[0:1, 0:1024], AF.Exp, scale=-1.0,
                                 inc="act")
                    sm_free = recip
                    for jj in range(2):
                        bc_pe = p.op(
                            "tensor", "matmul",
                            st[0][:, jj * 512:jj * 512 + 512], onesr[:],
                            ROWS[0:1, jj * 512:jj * 512 + 512], start=True,
                            stop=True, skip_group_check=True,
                            waits=[("act", recip), ("act", exp_done[0]),
                                   ("act", exp_done[1])] if jj == 0 else [],
                            inc="pe" if jj == 1 else None)
                    rbev = p.op("scalar", "copy", rb, st[0][:, 0:1024],
                                waits=[("pe", bc_pe)], inc="act")
                    exp_done[0] = rbev
                    osl = qi % 2
                    oun = p.op("scalar", "copy", ou[osl], av,
                               waits=[("pe", last_pe),
                                      ("dve", ou_rd[osl])], inc="act")
                    av_free = oun
                    osc = p.op("vector", "tensor_tensor", ob[osl], ou[osl],
                               rb, ALU.mult,
                               waits=[("act", rbev), ("act", oun),
                                      (f"ow{osl}", ow_n[osl])], inc="dve")
                    ou_rd[osl] = osc
                    ow_n[osl] = p.dma(
                        "sync", f"ow{osl}",
                        Od[h * 128:(h + 1) * 128, g * 1024:g * 1024 + 1024],
                        ob[osl], waits=[("dve", osc)])
                    qi += 1
                if g == 0:
                    state["od_g0"] = p.full("ow0", "ow1")
        state["od_done"] = p.full("ow0", "ow1")

    def emit_wo(layer, wsrc, xsrc, xdst, gate):
        oblk = pb(0, 16384)
        wslo = [pb(16384, 2048), pb(18432, 2048)]
        wslo_rd = [0, 0]
        ysl = [pb(20480, 512), pb(20992, 512)]
        yw_n = [0, 0]
        xo = [pf(2048, 512), pf(2560, 512)]
        wofree = [0, 0]
        od_done = state["od_done"]
        od_g0 = state["od_g0"]
        oblk_rd = 0
        for tp in range(2):
            for c in range(DC):
                w = ((od_g0 if tp == 0 else od_done)
                     + [("pe", oblk_rd)]) if c == 0 else []
                p.dma("sync", "obl", oblk[:, c * 1024:(c + 1) * 1024],
                      Od[c * 128:(c + 1) * 128, tp * 1024:tp * 1024 + 1024],
                      waits=w)
            oq = p.now("obl")
            for f in range(DC):
                i = f % 2
                w = [("pe", wslo_rd[i])]
                if tp == 0 and f == 0:
                    w += od_done
                lw = p.dma("gpsimd", f"wo{i}", wslo[i], wsrc[f], waits=w)

                for tb in range(2):
                    blk = tp * 2 + tb
                    sl = blk % 2
                    pp = psA[:, sl * 512:sl * 512 + 512]
                    for c in range(DC):
                        w = [(f"wo{i}", lw), ("obl", oq),
                             (("act", wofree[sl]) if layer == 0
                              else ("dve", wofree[sl]))] if c == 0 else []
                        pe_c = p.op(
                            "tensor", "matmul", pp,
                            wslo[i][:, c * 128:c * 128 + 128],
                            oblk[:, c * 1024 + tb * 512:
                                 c * 1024 + tb * 512 + 512],
                            start=(c == 0), stop=(c == DC - 1),
                            skip_group_check=True, waits=w,
                            inc="pe" if c == DC - 1 else None)
                    if tb == 1:
                        wslo_rd[i] = pe_c
                    if layer == 0:
                        ev = p.op("scalar", "copy", ysl[sl], pp,
                                  waits=[("pe", pe_c),
                                         (f"yw{sl}", yw_n[sl])], inc="act")
                        wofree[sl] = ev
                        yw_n[sl] = p.dma(
                            "sync", f"yw{sl}",
                            Yd[f * 128:(f + 1) * 128,
                               blk * 512:blk * 512 + 512], ysl[sl],
                            waits=[("act", ev)])
                    else:
                        w = [(f"xw{sl}", state[("xw", sl)])]
                        if tp == 0 and f == 0 and tb == 0:
                            w += gate
                        xq = p.dma("sync", f"xl{sl}", xo[sl],
                                   xsrc[f * 128:(f + 1) * 128,
                                        blk * 512:blk * 512 + 512], waits=w)
                        ad = p.op("vector", "tensor_tensor", xo[sl], xo[sl],
                                  pp, ALU.add,
                                  waits=[("pe", pe_c), (f"xl{sl}", xq)],
                                  inc="dve")
                        wofree[sl] = ad
                        state[("xw", sl)] = p.dma(
                            "sync", f"xw{sl}",
                            xdst[f * 128:(f + 1) * 128,
                                 blk * 512:blk * 512 + 512], xo[sl],
                            waits=[("dve", ad)])
            oblk_rd = p.now("pe")
            if tp == 0:
                state["yd_tp0"] = p.full("yw0", "yw1")
        return p.full("xw0", "xw1", "yw0", "yw1")

    def emit_rms(xsrc, xdst, gate):
        yblk = [pb(0, 8192), pb(10240, 8192)]
        ysq = [pb(8192, 512), pb(8704, 512)]
        rrb = pb(9216, 512)
        t3 = pb(9728, 512)
        xo = [pf(2048, 512), pf(2560, 512)]
        ysq_pe = [0, 0]
        ybq = [0, 0]
        yb_act = [0, 0]
        yb_dve = [0, 0]
        yd_tp0 = state["yd_tp0"]
        yd_done = state["yd_done"]
        rr_dve = 0
        streams = ["ybl", "yb2"]

        def load_yblk(tb, i):
            w = [("act", yb_act[i]), ("dve", yb_dve[i])]
            w += yd_tp0 if tb < 2 else yd_done
            for c in range(DC):
                p.dma("sync", streams[i], yblk[i][:, c * 512:(c + 1) * 512],
                      Yd[c * 128:(c + 1) * 128, tb * 512:tb * 512 + 512],
                      waits=w if c == 0 else [])
            return p.now(streams[i])

        ybq[0] = load_yblk(0, 0)
        for tb in range(NB):
            i = tb % 2
            if tb < NB - 1:
                ybq[1 - i] = load_yblk(tb + 1, 1 - i)
            for c in range(DC):
                ii = c % 2
                sq = p.op("scalar", "activation", ysq[ii],
                          yblk[i][:, c * 512:(c + 1) * 512], AF.Square,
                          waits=[(streams[i], ybq[i]), ("pe", ysq_pe[ii])],
                          inc="act")
                ysq_pe[ii] = p.op(
                    "tensor", "matmul", psA[0:1, 1024:1536], onesc[:],
                    ysq[ii], start=(c == 0), stop=(c == DC - 1),
                    skip_group_check=True,
                    waits=[("act", sq)] + ([("dve", rr_dve)] if c == 0
                                           else []), inc="pe")
            spe = p.now("pe")
            yb_act[i] = p.now("act")
            mr = ROWS[0:1, 0:512]
            lr = ROWS[0:1, 512:1024]
            d0 = p.op("vector", "tensor_scalar", mr, psA[0:1, 1024:1536],
                      1.0 / 2048.0, None, ALU.mult, waits=[("pe", spe)],
                      inc="dve")
            p.op("scalar", "activation", lr, mr, AF.Ln, bias=cb_sb[0:1, 2:3],
                 waits=[("dve", d0)])
            ra = p.op("scalar", "activation", mr, lr, AF.Exp, scale=-0.5,
                      inc="act")
            rpe = p.op("tensor", "matmul", psA[:, 1536:2048], onesr[:], mr,
                       start=True, stop=True, skip_group_check=True,
                       waits=[("act", ra), ("dve", d0)], inc="pe")
            rr_dve = p.op("vector", "tensor_copy", rrb, psA[:, 1536:2048],
                          waits=[("pe", rpe)], inc="dve")
            for c in range(DC):
                sl = c % 2
                w = [(f"xw{sl}", state[("xw", sl)])]
                if tb == 0 and c == 0:
                    w += gate
                xq = p.dma("sync", f"xl{sl}", xo[sl],
                           xsrc[c * 128:(c + 1) * 128,
                                tb * 512:tb * 512 + 512], waits=w)
                p.op("vector", "scalar_tensor_tensor", out=t3,
                     in0=yblk[i][:, c * 512:(c + 1) * 512],
                     scalar=nwa_sb[:, c:c + 1], in1=rrb,
                     op0=ALU.mult, op1=ALU.mult,
                     waits=[("dve", rr_dve)] + CONST_S)
                ad = p.op("vector", "tensor_tensor", xo[sl], xo[sl], t3,
                          ALU.add, waits=[(f"xl{sl}", xq)], inc="dve")
                state[("xw", sl)] = p.dma(
                    "sync", f"xw{sl}",
                    xdst[c * 128:(c + 1) * 128, tb * 512:tb * 512 + 512],
                    xo[sl], waits=[("dve", ad)])
            yb_dve[i] = p.now("dve")
        return p.full("xw0", "xw1")

    def emit_ffn(xln_dve, ffn_idx, xsrc, xdst, gate):
        ccs = "cf1" if ffn_idx == 0 else "cf2"
        lg = [psA[:, 0:12], psA[:, 512:524]]
        tp = [psA[0:12, 1024:1152], psA[0:12, 1536:1664]]
        Es = sel[:, 0:12]
        O1 = sel[:, 12:24]
        O2 = sel[:, 24:36]
        csA = sel[:, 36:48]
        csB = sel[:, 48:60]
        E2 = sel[:, 60:72]
        inv = sel[:, 72:84]
        m1 = rw_sb[:, 0:1]
        m1n = rw_sb[:, 1:2]
        m2e = rw_sb[:, 2:3]
        den = rw_sb[:, 3:4]
        rw1 = rw_sb[:, 4:5]
        rw2 = rw_sb[:, 5:6]
        lg_free = [(0, 0), (0, 0)]
        tp_free = [0, 0]
        for tt in range(16):
            i = tt % 2
            for c in range(DC):
                w = [("dve", xln_dve), ("dve", lg_free[i][0]),
                     ("act", lg_free[i][1])] + CONST_G if c == 0 else []
                lpe = p.op(
                    "tensor", "matmul", lg[i],
                    xln[:, c * 2048 + tt * 128:c * 2048 + tt * 128 + 128],
                    gw_sb[:, c * 12:c * 12 + 12], start=(c == 0),
                    stop=(c == DC - 1), skip_group_check=True, waits=w,
                    inc="pe" if c == DC - 1 else None)
            p.op("vector", "tensor_reduce", m1, lg[i], AX.X, ALU.max,
                 waits=[("pe", lpe)])
            p.op("vector", "tensor_scalar", m1n, m1, -1.0, None, ALU.mult)
            d = p.op("vector", "tensor_scalar", O1, lg[i], m1, None,
                     ALU.is_ge, inc="dve")
            sa = p.op("scalar", "activation", Es, lg[i], AF.Exp, bias=m1n,
                      waits=[("dve", d)], inc="act")
            p.op("vector", "tensor_tensor_scan", out=csA, data0=O1,
                 data1=O1, initial=0.0, op0=ALU.add, op1=ALU.max)
            p.op("vector", "tensor_tensor", csB, csA, O1, ALU.subtract)
            p.op("vector", "tensor_scalar", csB, csB, 0.5, None, ALU.is_lt)
            p.op("vector", "tensor_tensor", O1, O1, csB, ALU.mult)
            p.op("vector", "tensor_scalar", inv, O1, -1.0, 1.0, ALU.mult,
                 ALU.add)
            p.op("vector", "tensor_tensor", E2, Es, inv, ALU.mult,
                 waits=[("act", sa)])
            p.op("vector", "tensor_reduce", m2e, E2, AX.X, ALU.max)
            p.op("vector", "tensor_scalar", O2, E2, m2e, None, ALU.is_ge)
            p.op("vector", "tensor_tensor_scan", out=csA, data0=O2,
                 data1=O2, initial=0.0, op0=ALU.add, op1=ALU.max)
            p.op("vector", "tensor_tensor", csB, csA, O2, ALU.subtract)
            p.op("vector", "tensor_scalar", csB, csB, 0.5, None, ALU.is_lt)
            p.op("vector", "tensor_tensor", O2, O2, csB, ALU.mult)
            p.op("vector", "tensor_scalar", den, m2e, 1.0, None, ALU.add)
            p.op("vector", "reciprocal", rw1, den)
            p.op("vector", "tensor_scalar", rw2, rw1, -1.0, 1.0, ALU.mult,
                 ALU.add)
            p.op("vector", "tensor_copy", OS1[:, tt * 12:tt * 12 + 12], O1)
            p.op("vector", "tensor_copy", OS2[:, tt * 12:tt * 12 + 12], O2)
            p.op("vector", "tensor_copy", RWS[:, 2 * tt:2 * tt + 1], rw1)
            d1 = p.op("vector", "tensor_copy", RWS[:, 2 * tt + 1:2 * tt + 2],
                      rw2, inc="dve")
            lg_free[i] = (d1, sa)
            tr1 = p.op("tensor", "transpose", tp[i][0:12, 0:128], O1,
                       id_sb[:], waits=[("dve", d1), ("dve", tp_free[i])]
                       + CONST_S, inc="pe")
            e1 = p.op("vector", "tensor_copy",
                      RT1[:, tt * 128:tt * 128 + 128], tp[i][0:12, :],
                      waits=[("pe", tr1)], inc="dve")
            tr2 = p.op("tensor", "transpose", tp[i][0:12, 0:128], O2,
                       id_sb[:], waits=[("dve", e1)], inc="pe")
            tp_free[i] = p.op("vector", "tensor_copy",
                              RT2[:, tt * 128:tt * 128 + 128],
                              tp[i][0:12, :], waits=[("pe", tr2)], inc="dve")
        sel_done = p.now("dve")
        for bpos in range(4):
            for s in range(2):
                RT = RT1 if s == 0 else RT2
                u = bpos * 2 + s
                cd = p.op("vector", "tensor_reduce", cnts_sb[:, u:u + 1],
                          RT[:, bpos * 512:bpos * 512 + 512], AX.X, ALU.add,
                          inc="dve" if (bpos == 3 and s == 1) else None)
        cq = p.dma("sync", "cw",
                   contrib.ap().rearrange("o (u e) -> (o e) u", u=8, e=12),
                   cnts_sb[:], waits=[("dve", cd)],
                   allow_slow_non_contiguous=True)
        p.op("gpsimd", "collective_compute", "AllGather", ALU.bypass,
             replica_groups=[list(range(8))],
             ins=[contrib.ap().opt()], outs=[gath.ap().opt()],
             waits=[("cw", cq)], inc=ccs)
        M = PF[0:12, 0:2048]
        m_dve = [0]

        def emit_mphase():
            gq = p.dma("sync", "gl", g64_sb[:],
                       gath.ap().rearrange("r (u e) -> (r u) e", u=8, e=12),
                       waits=[(ccs, p.now(ccs))])
            bmm = p.op("tensor", "matmul", psA[0:12, 2048:2056], g64_sb[:],
                       prec_sb[:], start=True, stop=True,
                       skip_group_check=True,
                       waits=[("gl", gq)] + CONST_S, inc="pe")
            thr_dve = p.op("vector", "tensor_scalar", thr_sb[:],
                           psA[0:12, 2048:2056], -1.0, float(CAP) + 0.5,
                           ALU.mult, ALU.add, waits=[("pe", bmm)], inc="dve")
            kp1 = psA[:, 1024:1036]
            kp2 = psA[:, 1536:1548]
            mp = psA[0:12, 2048:2176]
            ktm1 = sel[:, 36:48]
            ktm2 = sel[:, 48:60]
            tA = sel[:, 60:72]
            tB = sel[:, 72:84]
            kp_dve = [0, 0]
            mp_dve = 0
            for bpos in range(4):
                p.op("vector", "tensor_tensor_scan", out=rs_sb[:],
                     data0=RT1[:, bpos * 512:bpos * 512 + 512],
                     data1=RT1[:, bpos * 512:bpos * 512 + 512],
                     initial=0.0, op0=ALU.add, op1=ALU.max,
                     waits=[("dve", thr_dve)])
                p.op("vector", "tensor_scalar", rs_sb[:], rs_sb[:],
                     thr_sb[:, 2 * bpos:2 * bpos + 1], None, ALU.is_lt)
                p.op("vector", "tensor_tensor_scan", out=rs2_sb[:],
                     data0=RT2[:, bpos * 512:bpos * 512 + 512],
                     data1=RT2[:, bpos * 512:bpos * 512 + 512],
                     initial=0.0, op0=ALU.add, op1=ALU.max)
                kd = p.op("vector", "tensor_scalar", rs2_sb[:], rs2_sb[:],
                          thr_sb[:, 2 * bpos + 1:2 * bpos + 2], None,
                          ALU.is_lt, inc="dve")
                for q in range(4):
                    tt = bpos * 4 + q
                    tk1 = p.op("tensor", "transpose", kp1,
                               rs_sb[:, q * 128:q * 128 + 128],
                               id_sb[0:12, 0:12],
                               waits=[("dve", kd), ("dve", kp_dve[0])],
                               inc="pe")
                    tk2 = p.op("tensor", "transpose", kp2,
                               rs2_sb[:, q * 128:q * 128 + 128],
                               id_sb[0:12, 0:12],
                               waits=[("dve", kp_dve[1])], inc="pe")
                    p.op("vector", "tensor_copy", ktm1, kp1,
                         waits=[("pe", tk1)])
                    kp_dve[0] = p.op("vector", "tensor_copy", ktm2, kp2,
                                     waits=[("pe", tk2)], inc="dve")
                    kp_dve[1] = kp_dve[0]
                    p.op("vector", "tensor_tensor", tA, ktm1,
                         OS1[:, tt * 12:tt * 12 + 12], ALU.mult)
                    p.op("vector", "tensor_scalar", tA, tA,
                         RWS[:, 2 * tt:2 * tt + 1], None, ALU.mult)
                    p.op("vector", "tensor_tensor", tB, ktm2,
                         OS2[:, tt * 12:tt * 12 + 12], ALU.mult)
                    p.op("vector", "tensor_scalar", tB, tB,
                         RWS[:, 2 * tt + 1:2 * tt + 2], None, ALU.mult)
                    md = p.op("vector", "tensor_tensor", tA, tA, tB, ALU.add,
                              inc="dve")
                    tm = p.op("tensor", "transpose", mp, tA, id_sb[:],
                              waits=[("dve", md), ("dve", mp_dve)], inc="pe")
                    mp_dve = p.op("vector", "tensor_copy",
                                  M[:, tt * 128:tt * 128 + 128], mp,
                                  waits=[("pe", tm)], inc="dve")
            m_dve[0] = mp_dve

        w13s = [pb(0, 4096), pb(4096, 4096), pb(8192, 4096), pb(12288, 4096)]
        w13_n = [0, 0, 0, 0]
        w13_rd = [0, 0, 0, 0]
        s1 = pb(16384, 1024)
        g1 = pb(17408, 1024)
        mbf = pf(2048, 1024)
        w2sl = [pb(18432, 3072), pb(21504, 3072)]
        w2_nl = [0, 0]
        w2_rdl = [0, 0]
        h1 = psA[:, 0:1024]
        h3 = psA[:, 1024:2048]
        mb = psA[:, 2048:3072]
        mrow_ps = psA[0:1, 3072:4096]
        h_free = 0
        mb_free = 0
        mrow_rd = 0
        gs_rd = 0
        py_free = [0, 0]
        first_h = True
        for g in range(2):
            for e in range(NE):
                sA = 2 * (e % 2)
                w = [("pe", w13_rd[sA])]
                if first_h:
                    w += [("dve", xln_dve), ("dve", sel_done)]
                w13_n[sA] = p.dma("gpsimd", "w130" if sA == 0 else "w131",
                                  pb(sA * 4096, 8192),
                                  w13[e], waits=w + [("pe", w13_rd[sA + 1])])
                w13_n[sA + 1] = w13_n[sA]
                for fc in range(2):
                    for half in range(2):
                        for c in range(DC):
                            w = []
                            if half == 0 and c == 0:
                                w = [("act", h_free)]
                                if fc == 0:
                                    w += [("w130" if sA == 0 else "w131",
                                           w13_n[sA]),
                                          ("dve", xln_dve)]
                                    if first_h:
                                        w += [("dve", sel_done)]
                                    prev = state.get(("gsrd", e), 0)
                                    if prev:
                                        w.append(("pe", prev))
                            p.op("tensor", "matmul",
                                 h1[:, half * 512:half * 512 + 512],
                                 w13s[sA][:, (fc * DC + c) * 128:
                                          (fc * DC + c) * 128 + 128],
                                 xln[:, c * 2048 + g * 1024 + half * 512:
                                     c * 2048 + g * 1024 + half * 512 + 512],
                                 start=(c == 0), stop=(c == DC - 1),
                                 skip_group_check=True, waits=w)
                    for half in range(2):
                        for c in range(DC):
                            hpe = p.op(
                                "tensor", "matmul",
                                h3[:, half * 512:half * 512 + 512],
                                w13s[sA + 1][:, (fc * DC + c) * 128:
                                             (fc * DC + c) * 128 + 128],
                                xln[:, c * 2048 + g * 1024 + half * 512:
                                    c * 2048 + g * 1024 + half * 512 + 512],
                                start=(c == 0), stop=(c == DC - 1),
                                skip_group_check=True,
                                inc="pe" if (half == 1 and c == DC - 1)
                                else None)
                    if fc == 1:
                        w13_rd[sA] = hpe
                        w13_rd[sA + 1] = hpe
                    ev1 = p.op("scalar", "copy", s1, h1,
                               waits=[("pe", hpe), ("dve", gs_rd)],
                               inc="act")
                    p1 = p.op("vector", "tensor_tensor", g1, s1, h3,
                              ALU.mult, waits=[("act", ev1)], inc="dve")
                    sact = p.op("scalar", "activation", s1, g1, AF.Silu,
                                waits=[("dve", p1)], inc="act")
                    gd = p.op("vector", "tensor_copy",
                              GS[:, (e * 2 + fc) * 1024:
                                 (e * 2 + fc) * 1024 + 1024], s1,
                              waits=[("act", sact)], inc="dve")
                    gs_rd = gd
                    h_free = p.op("scalar", "copy", rw_sb[:, 6:7],
                                  cb_sb[:, 3:4], waits=[("dve", gd)],
                                  inc="act")
                first_h = False
            h_all = p.now("dve")
            if g == 0:
                emit_mphase()
            gate_d = p.now("dve")
            gate_x = p.full("xw0", "xw1")
            for e in range(NE):
                for jj in range(2):
                    w = []
                    if jj == 0:
                        w = [("dve", m_dve[0]), ("dve", mrow_rd),
                             ("dve", gate_d)]
                    mpe = p.op("tensor", "matmul",
                               mrow_ps[0:1, jj * 512:jj * 512 + 512],
                               oh_sb[:, e:e + 1],
                               M[:, g * 1024 + jj * 512:
                                 g * 1024 + jj * 512 + 512],
                               start=True, stop=True, skip_group_check=True,
                               waits=w)
                mrow_rd = p.op("vector", "tensor_copy", ROWS[0:1, 1024:2048],
                               mrow_ps, waits=[("pe", mpe)], inc="dve")
                for jj in range(2):
                    w = [("dve", mrow_rd), ("dve", mb_free)] if jj == 0 else []
                    mbpe = p.op(
                        "tensor", "matmul", mb[:, jj * 512:jj * 512 + 512],
                        onesr[:],
                        ROWS[0:1, 1024 + jj * 512:1024 + jj * 512 + 512],
                        start=True, stop=True, skip_group_check=True,
                        waits=w, inc="pe" if jj == 1 else None)
                mbd = p.op("vector", "tensor_copy", mbf, mb,
                           waits=[("pe", mbpe)] + gate_x, inc="dve")
                mb_free = mbd
                for fc in range(2):
                    sc = p.op("vector", "tensor_tensor",
                              GS[:, (e * 2 + fc) * 1024:
                                 (e * 2 + fc) * 1024 + 1024],
                              GS[:, (e * 2 + fc) * 1024:
                                 (e * 2 + fc) * 1024 + 1024], mbf,
                              ALU.mult, waits=[("dve", mbd)], inc="dve")
            gs_scaled = p.now("dve")
            for f in range(16):
                iw = f % 2
                w = [("pe", w2_rdl[iw])]
                if f == 0:
                    w.append(("dve", gs_scaled))
                w2_nl[iw] = p.dma("gpsimd", "w132" if iw == 0 else "w133",
                                  w2sl[iw], w2p[f], waits=w)
                if True:
                    for blk in range(2):
                        tb = g * 2 + blk
                        sl = blk % 2
                        py = psA[:, 3072 + sl * 512:3072 + sl * 512 + 512]
                        for ef in range(2 * NE):
                            e, fc = ef // 2, ef % 2
                            w = []
                            if ef == 0:
                                w = [("w132" if iw == 0 else "w133",
                                      w2_nl[iw]), ("dve", py_free[sl]),
                                     ("dve", mrow_rd),
                                     ("dve", gs_scaled)]
                            ype = p.op(
                                "tensor", "matmul", py,
                                w2sl[iw][:, ef * 128:ef * 128 + 128],
                                GS[:, ef * 1024 + blk * 512:
                                   ef * 1024 + blk * 512 + 512],
                                start=(ef == 0),
                                stop=(ef == 2 * NE - 1),
                                skip_group_check=True, waits=w,
                                inc="pe" if ef == 2 * NE - 1
                                else None)
                        i2 = (f * 2 + blk) % 2
                        xof = pf(2048 + i2 * 512, 512)
                        xw = [(f"xw{i2}", state[("xw", i2)])]
                        if f == 0 and blk == 0:
                            xw.append(("dve", gs_scaled))
                            if g == 0:
                                xw += gate
                        xq = p.dma("sync", f"xl{i2}", xof,
                                   xsrc[f * 128:(f + 1) * 128,
                                        tb * 512:tb * 512 + 512], waits=xw)
                        ad = p.op("vector", "tensor_tensor", xof, xof, py,
                                  ALU.add, waits=[("pe", ype),
                                                  (f"xl{i2}", xq)],
                                  inc="dve")
                        py_free[sl] = ad
                        state[("xw", i2)] = p.dma(
                            "sync", f"xw{i2}",
                            xdst[f * 128:(f + 1) * 128,
                                 tb * 512:tb * 512 + 512], xof,
                            waits=[("dve", ad)])
                    w2_rdl[iw] = p.now("pe")
            for e in range(NE):
                state[("gsrd", e)] = p.now("pe")
        return p.full("xw0", "xw1")

    def emit_attn(layer, xsrc, xdst, gate):
        xd = emit_ln(xsrc, gate)
        wsl_rd[0] = wsl_rd[1] = p.now("pe")
        cs_src = CSa if layer == 0 else CSb
        CS = pb(16384, 4096)
        p.dma("gpsimd", "cs", CS[:, 0:2048], cs_src[0], waits=gate)
        p.dma("gpsimd", "cs", CS[:, 2048:4096], cs_src[1])
        RP["cs_n"] = p.now("cs")
        RP["pr_w"] = []
        RP["tq_w"] = [[], []]
        RP["qo_w"] = [[], []]
        wk_src = wak if layer == 0 else wbk
        wq_src = waq if layer == 0 else wbq
        wv_src = wav if layer == 0 else wbv
        wo_src = wpa if layer == 0 else wbo

        def kdst(j):
            def f(blk):
                return kown.ap()[j * 128:(j + 1) * 128,
                                 blk * 512:blk * 512 + 512]
            return f

        def qdst(h):
            def f(blk):
                return Qd[h * 128:(h + 1) * 128, blk * 512:blk * 512 + 512]
            return f

        for j in range(NKV):
            proj_head(j, wk_src[j], xd, rope_epilog(layer, kdst(j), "kw"))
        kq = p.full("kw0", "kw1")
        vq = emit_vproj(wv_src, xd)
        ck, cv = ("cka", "cva") if layer == 0 else ("ckb", "cvb")
        p.op("gpsimd", "collective_compute", "AllGather", ALU.bypass,
             replica_groups=[[0, 1], [2, 3], [4, 5], [6, 7]],
             ins=[kown.ap().opt()], outs=[kg.ap().opt()],
             waits=kq + vq, inc=ck)
        p.op("gpsimd", "collective_compute", "AllGather", ALU.bypass,
             replica_groups=[[0, 1], [2, 3], [4, 5], [6, 7]],
             ins=[vown.ap().opt()], outs=[vg.ap().opt()], inc=cv)
        for h in range(NH):
            proj_head(h, wq_src[h], xd, rope_epilog(layer, qdst(h), "qw"))
        state["qd_done"] = p.full("qw0", "qw1")
        emit_core(layer)
        if layer == 0:
            emit_wo(0, wo_src, None, None, gate)
            state["yd_done"] = p.full("yw0", "yw1")
            return emit_rms(xsrc, xdst, gate)
        else:
            return emit_wo(1, wo_src, xsrc, xdst, gate)

    end = emit_attn(0, x_fm, Xd, CONST_S + CONST_G)
    xd = emit_ln(Xd, end)
    end = emit_ffn(xd, 0, Xd, Xd, end)
    end = emit_attn(1, Xd, Xd, end)
    xd = emit_ln(Xd, end)
    end = emit_ffn(xd, 1, Xd, y_fm, end)
    p.op("sync", "nop", waits=[(s, p.now(s)) for s in SEMS])

    sems = {}
    with ExitStack() as sst:
        for s in SEMS:
            sems[s] = sst.enter_context(nc.semaphore(s))
        with nc.Block() as block:
            block.sync(_replay(p, sems, "sync"))
            block.gpsimd(_replay(p, sems, "gpsimd"))
            block.tensor(_replay(p, sems, "tensor"))
            block.scalar(_replay(p, sems, "scalar"))
            block.vector(_replay(p, sems, "vector"))
    st_ctx.close()
    return nc


def _perm_a():
    pm = np.zeros(128, np.int64)
    pm[0:64] = np.arange(0, 128, 2)
    pm[64:128] = np.arange(1, 128, 2)
    return pm


def _perm_b():
    pm = np.zeros(128, np.int64)
    pm[0:32] = np.arange(0, 64, 2)
    pm[32:64] = np.arange(1, 64, 2)
    pm[64:128] = np.arange(64, 128)
    return pm


def _slab_pack(w):
    Dd, Ff = w.shape
    return np.ascontiguousarray(
        w.reshape(Dd // 128, 128, Ff // 128, 128).transpose(2, 1, 0, 3)
        .reshape(Ff // 128, 128, Dd))


def _vslab_pack(wv):
    out = np.zeros((2, 128, 4096), np.float32)
    for u in range(2):
        for c in range(16):
            for jj in range(2):
                j = 2 * u + jj
                out[u, :, c * 256 + jj * 128:c * 256 + jj * 128 + 128] = \
                    wv[c * 128:(c + 1) * 128, j * 128:(j + 1) * 128]
    return np.ascontiguousarray(out)


def _my_tokens(cls):
    idx = []
    for b in BLOCKS[cls]:
        idx.extend(range(b * 512, (b + 1) * 512))
    return np.array(idx, np.int64)


def _rope_tables(kind, pos):
    pos = pos.astype(np.float64)
    C = np.zeros((128, NT))
    S = np.zeros((128, NT))
    if kind == 0:
        inv = 10000.0 ** (-np.arange(0, 128, 2, dtype=np.float64) / 128.0)
        ang = inv[:, None] * pos[None, :]
        C[0:64] = np.cos(ang)
        C[64:128] = np.cos(ang)
        S[0:64] = -np.sin(ang)
        S[64:128] = np.sin(ang)
    else:
        inv = 10000.0 ** (-np.arange(0, 64, 2, dtype=np.float64) / 64.0)
        ang = inv[:, None] * pos[None, :]
        C[0:32] = np.cos(ang)
        C[32:64] = np.cos(ang)
        C[64:128] = 1.0
        S[0:32] = -np.sin(ang)
        S[32:64] = np.sin(ang)
    return C.astype(np.float32), S.astype(np.float32)


def _prm_a_mat():
    m = np.zeros((128, 128), np.float32)
    for i in range(64):
        m[64 + i, i] = 1.0
        m[i, 64 + i] = 1.0
    return m


def _prm_b_mat():
    m = np.zeros((128, 128), np.float32)
    for i in range(32):
        m[32 + i, i] = 1.0
        m[i, 32 + i] = 1.0
    return m


def _prec_fill(core):
    def key(r, pp, s):
        return (s, r // 2, BLOCKS[r % 2][pp])
    out = np.zeros((64, 8), np.float32)
    for r in range(8):
        for u in range(8):
            pp, s = u // 2, u % 2
            ku = key(r, pp, s)
            for mu in range(8):
                mp, ms = mu // 2, mu % 2
                if ku < key(core, mp, ms):
                    out[r * 8 + u, mu] = 1.0
    return out


def kernel(L0_attn_a_x, L0_attn_a_w_attn, L0_attn_a_w_proj, L0_attn_a_norm_w,
           L0_attn_b_Wq, L0_attn_b_Wk, L0_attn_b_Wv, L0_attn_b_Wo,
           L0_ffn_gate_w, L0_ffn_w1, L0_ffn_w2, L0_ffn_w3):
    x = np.asarray(L0_attn_a_x, np.float32)
    wa = np.asarray(L0_attn_a_w_attn, np.float32)
    wpa_ = np.asarray(L0_attn_a_w_proj, np.float32)
    nw = np.asarray(L0_attn_a_norm_w, np.float32)
    wq = np.asarray(L0_attn_b_Wq, np.float32)
    wk = np.asarray(L0_attn_b_Wk, np.float32)
    wv = np.asarray(L0_attn_b_Wv, np.float32)
    wo = np.asarray(L0_attn_b_Wo, np.float32)
    gwt = np.asarray(L0_ffn_gate_w, np.float32)
    w1 = np.asarray(L0_ffn_w1, np.float32)
    w2 = np.asarray(L0_ffn_w2, np.float32)
    w3 = np.asarray(L0_ffn_w3, np.float32)

    pa, pbm = _perm_a(), _perm_b()
    waq_full = wa[:, 0:2048].reshape(D, NH, 128)[:, :, pa].reshape(D, 2048)
    wak_full = wa[:, 2048:2560].reshape(D, NKV, 128)[:, :, pa].reshape(D, 512)
    wbq_full = wq.reshape(D, NH, 128)[:, :, pbm].reshape(D, 2048)
    wbk_full = wk.reshape(D, NKV, 128)[:, :, pbm].reshape(D, 512)

    shared = dict(
        waq=_slab_pack(waq_full),
        wak=_slab_pack(wak_full),
        wav=_vslab_pack(wa[:, 2560:3072]),
        wpa=_slab_pack(wpa_),
        nwa=np.ascontiguousarray(nw.reshape(DC, 128).T),
        wbq=_slab_pack(wbq_full),
        wbk=_slab_pack(wbk_full),
        wbv=_vslab_pack(wv),
        wbo=_slab_pack(wo),
        gw=np.ascontiguousarray(
            gwt.T.reshape(DC, 128, NE).transpose(1, 0, 2).reshape(
                128, DC * NE)),
        idn=np.eye(128, dtype=np.float32),
        onescol=np.ones((128, 1), np.float32),
        onesrow=np.ones((1, 128), np.float32),
        oh12=np.eye(12, dtype=np.float32),
        cbias=np.ascontiguousarray(np.tile(
            np.array([EXPB, 1e-5, 1e-6, 0.0], np.float32), (128, 1))),
        prm=np.stack([_prm_a_mat(), _prm_b_mat()]),
    )
    w13p = np.zeros((NE, 128, 8192), np.float32)
    for e in range(NE):
        for k_, wsrc in ((0, w1), (1, w3)):
            wt = wsrc[e].T
            s = wt.reshape(DC, 128, 2, 128)
            w13p[e, :, k_ * 4096:(k_ + 1) * 4096] = \
                s.transpose(1, 2, 0, 3).reshape(128, 4096)
    shared["w13"] = np.ascontiguousarray(w13p)
    w2pp = np.zeros((16, 128, 3072), np.float32)
    for e in range(NE):
        wt = w2[e].T
        for f in range(16):
            for fc in range(2):
                w2pp[f, :, (e * 2 + fc) * 128:(e * 2 + fc) * 128 + 128] = \
                    wt[fc * 128:(fc + 1) * 128, f * 128:(f + 1) * 128]
    shared["w2p"] = np.ascontiguousarray(w2pp)

    korder = np.concatenate([_my_tokens(0), _my_tokens(1)])
    kidc = np.ascontiguousarray(korder.reshape(32, 128).T.astype(np.float32))

    nc = build_nc()
    in_maps = []
    for core in range(8):
        b, cls = core // 2, core % 2
        toks = _my_tokens(cls)
        pos = toks.astype(np.float32)
        Ca, Sa = _rope_tables(0, pos)
        Cb, Sb = _rope_tables(1, pos)
        m = dict(shared)
        m["x_fm"] = np.ascontiguousarray(x[b][toks].T)
        m["csa"] = np.ascontiguousarray(np.stack([Ca, Sa]))
        m["csb"] = np.ascontiguousarray(np.stack([Cb, Sb]))
        m["qidc"] = np.ascontiguousarray(np.tile(pos[None, :], (128, 1)))
        m["kidc"] = kidc
        m["prec"] = _prec_fill(core)
        in_maps.append(m)

    res = run_bass_kernel_spmd(nc, in_maps, core_ids=list(range(8)))
    out = np.empty((4, 4096, D), np.float32)
    for core in range(8):
        b, cls = core // 2, core % 2
        out[b][_my_tokens(cls)] = res.results[core]["y_fm"].T
    return out


if __name__ == "__main__":
    print("building program...")
    nc = build_nc()
    print("instructions:", nc.n_instructions())
    print("sbuf bytes remaining:", nc.sbuf_bytes_remaining)


# version 16
# speedup vs baseline: 1.1410x; 1.1410x over previous
import numpy as np
from contextlib import ExitStack

import concourse.bass as bass
import concourse.mybir as mybir
from concourse.bass_utils import run_bass_kernel_spmd

F32 = mybir.dt.float32
BF16 = mybir.dt.bfloat16
FP16 = mybir.dt.float16
AF = mybir.ActivationFunctionType
ALU = mybir.AluOpType
AX = mybir.AxisListType

D = 2048
DC = 16
NT = 2048
NB = 4
NH = 16
NKV = 4
HD = 128
NE = 12
CAP = 1707
SCALE = 1.0 / float(np.sqrt(128.0))
EXPB = -20.0

BLOCKS = [[0, 3, 4, 7], [1, 2, 5, 6]]


def _tile_seq():
    out = []
    for c in range(2):
        for b in BLOCKS[c]:
            for q in range(4):
                out.append((b * 512 + q * 128, b * 512 + q * 128 + 128))
    return out


TILE_SEQ = _tile_seq()
TILES = []
for g in range(2):
    row = []
    for h in range(2):
        sbs = [BLOCKS[c][2 * g + h] for c in range(2)]
        lst = []
        for t, (lo, hi) in enumerate(TILE_SEQ):
            if any(lo < (sb + 1) * 512 for sb in sbs):
                need_mask = any(hi > sb * 512 + 1 for sb in sbs)
                lst.append((t, need_mask))
        row.append(lst)
    TILES.append(row)

SEMS = ("pe", "act", "dve",
        "sc", "ql0", "ql1", "kv0", "kv1", "ow0", "ow1", "qw0", "qw1",
        "kw0", "kw1", "vw", "obl", "ybl", "yw0", "yw1", "xl0", "xl1",
        "xw0", "xw1", "cw", "gl",
        "gc", "xb0", "xb1", "ws0", "ws1", "wo0", "wo1",
        "w130", "w131", "w132", "w133", "w2l", "cs", "yb2", "pla",
        "cka", "cva", "ckb", "cvb", "cf1", "cf2")


class Prog:
    ENGINES = ("sync", "gpsimd", "tensor", "scalar", "vector")

    def __init__(self):
        self.items = {e: [] for e in self.ENGINES}
        self.counts = {}
        self.waited = {}

    def sem(self, name):
        self.counts[name] = 0

    def now(self, s):
        return self.counts[s]

    AUTOSEM = {"tensor": "pe", "scalar": "act", "vector": "dve"}
    POOL_OPS = ("tensor_tensor", "tensor_copy", "tensor_scalar")

    def op(self, eng, method, *args, waits=(), inc=None, inc_by=1, **kw):
        if inc is None and method != "nop":
            inc = self.AUTOSEM.get(eng)
            if eng == "gpsimd" and method in self.POOL_OPS:
                inc = "pla"
        waits = list(waits)
        if eng in ("vector", "scalar") and method != "nop":
            es = self.AUTOSEM[eng]
            waits.append((es, self.counts[es]))
        elif eng == "gpsimd" and method in self.POOL_OPS:
            waits.append(("pla", self.counts["pla"]))
        ws = []
        seen = self.waited.setdefault(eng, {})
        for s, v in waits:
            if v <= 0 or v <= seen.get(s, 0):
                continue
            assert v <= self.counts[s], (eng, method, s, v, self.counts[s])
            seen[s] = v
            ws.append((s, v))
        if inc is not None:
            self.counts[inc] += inc_by
        self.items[eng].append((method, args, kw, ws,
                                (inc, inc_by) if inc else None))
        return self.counts[inc] if inc is not None else None

    def dma(self, eng, sem, out, in_, waits=(), **kw):
        return self.op(eng, "dma_start", out=out, in_=in_, waits=waits,
                       inc=sem, inc_by=16, **kw)

    def full(self, *names):
        return [(s, self.counts[s]) for s in names]


def _replay(prog, sems, engname):
    def run(eng):
        for method, args, kw, ws, inc in prog.items[engname]:
            for s, v in ws:
                eng.wait_ge(sems[s], v)
            inst = getattr(eng, method)(*args, **kw)
            if inc is not None:
                inst.then_inc(sems[inc[0]], inc[1])
    return run


def build_nc():
    nc = bass.Bass("TRN2", target_bir_lowering=False, debug=False)

    def din(name, shape, dt=F32):
        return nc.dram_tensor(name, shape, dt, kind="ExternalInput").ap()

    x_fm = din("x_fm", [D, NT])
    waq = din("waq", [NH, 128, D])
    wak = din("wak", [NKV, 128, D])
    wav = din("wav", [2, 128, 4096])
    wpa = din("wpa", [DC, 128, D])
    nwa = din("nwa", [128, DC])
    wbq = din("wbq", [NH, 128, D])
    wbk = din("wbk", [NKV, 128, D])
    wbv = din("wbv", [2, 128, 4096])
    wbo = din("wbo", [DC, 128, D])
    gw = din("gw", [128, DC * NE])
    w13 = din("w13", [NE, 128, 8192])
    w2p = din("w2p", [16, 128, 3072])
    CSa = din("csa", [2, 128, NT])
    CSb = din("csb", [2, 128, NT])
    prm = din("prm", [2, 128, 128])
    qidc = din("qidc", [128, NT])
    kidc = din("kidc", [128, 32])
    idn = din("idn", [128, 128])
    onescol = din("onescol", [128, 1])
    onesrow = din("onesrow", [1, 128])
    oh12 = din("oh12", [12, 12])
    prec = din("prec", [64, 8])
    cbias = din("cbias", [128, 4])

    y_fm = nc.dram_tensor("y_fm", [D, NT], F32, kind="ExternalOutput").ap()

    Xd = nc.dram_tensor("Xd", [D, NT], F32).ap()
    Qd = nc.dram_tensor("Qd", [D, NT], BF16).ap()
    Od = nc.dram_tensor("Od", [D, NT], BF16).ap()
    Yd = nc.dram_tensor("Yd", [D, NT], BF16).ap()
    kown = nc.dram_tensor("kown", [512, NT], BF16)
    vown = nc.dram_tensor("vown", [512, NT], BF16)
    kg = nc.dram_tensor("kg", [1024, NT], BF16)
    vg = nc.dram_tensor("vg", [1024, NT], BF16)
    contrib = nc.dram_tensor("contrib", [1, 96], F32)
    gath = nc.dram_tensor("gath", [8, 96], F32)

    p = Prog()
    for s in SEMS:
        p.sem(s)

    st_ctx = ExitStack()
    sb = lambda n, sh, dt: st_ctx.enter_context(nc.sbuf_tensor(n, sh, dt))

    xln = sb("xln", [128, DC * NT], BF16)
    PB = sb("PB", [128, 24576], BF16)
    PF = sb("PF", [128, 3072], F32)
    GS = sb("GS", [128, 24576], BF16)
    ROWS = sb("ROWS", [1, 4096], F32)
    RT1 = sb("RT1", [12, NT], FP16)
    RT2 = sb("RT2", [12, NT], FP16)
    OS1 = sb("OS1", [128, 192], BF16)
    OS2 = sb("OS2", [128, 192], BF16)
    id_bf = sb("id_bf", [128, 128], BF16)
    RWS = sb("RWS", [128, 32], F32)
    rs2_sb = sb("rs2_sb", [12, 512], F32)
    id_sb = sb("id_sb", [128, 128], F32)
    prm_sb = sb("prm_sb", [128, 256], BF16)
    onesc = sb("onesc", [128, 1], BF16)
    onesc32 = sb("onesc32", [128, 1], F32)
    onesr = sb("onesr", [1, 128], F32)
    oh_sb = sb("oh_sb", [12, 12], F32)
    prec_sb = sb("prec_sb", [64, 8], F32)
    gw_sb = sb("gw_sb", [128, DC * NE], BF16)
    nwa_sb = sb("nwa_sb", [128, DC], F32)
    cb_sb = sb("cb_sb", [128, 4], F32)
    kid_sb = sb("kid_sb", [128, 32], F32)
    g64_sb = sb("g64_sb", [64, 12], F32)
    cnts_sb = sb("cnts_sb", [12, 8], F32)
    thr_sb = sb("thr_sb", [12, 8], F32)
    rs_sb = sb("rs_sb", [12, 512], F32)
    sel = sb("sel", [128, 96], F32)
    rw_sb = sb("rw_sb", [128, 8], F32)
    psA = st_ctx.enter_context(nc.psum_tensor("psA", [128, 4096], F32))

    def pb(o, n):
        return PB[:, o:o + n]

    def pf(o, n):
        return PF[:, o:o + n]

    qidc_sb = pf(0, 2048)

    for t, src in ((qidc_sb, qidc), (id_sb[:], idn), (onesr[:], onesrow),
                   (oh_sb[:], oh12), (prec_sb[:], prec), (nwa_sb[:], nwa),
                   (cb_sb[:], cbias), (kid_sb[:], kidc),
                   (onesc32[:], onescol)):
        p.dma("sync", "sc", t, src)
    for t, src in ((prm_sb[:, 0:128], prm[0]), (prm_sb[:, 128:256], prm[1]),
                   (onesc[:], onescol), (gw_sb[:], gw), (id_bf[:], idn)):
        p.dma("gpsimd", "gc", t, src)
    CONST_S = p.full("sc")
    CONST_G = p.full("gc")

    state = {("xw", 0): 0, ("xw", 1): 0}

    def emit_ln(xsrc, gate):
        xb = [pb(0, 2048), pb(2048, 2048)]
        xq = [pb(4096, 2048), pb(6144, 2048)]
        xb_pe = [0, 0]
        xb_act = [0, 0]
        xq_pe = [0, 0]
        for c in range(DC):
            i = c % 2
            w = [("pe", xb_pe[i]), ("act", xb_act[i])]
            if c == 0:
                w += gate
            ld = p.dma("gpsimd", f"xb{i}", xb[i],
                       xsrc[c * 128:(c + 1) * 128, :], waits=w)
            xb_act[i] = p.op("scalar", "activation", xq[i], xb[i], AF.Square,
                             waits=[(f"xb{i}", ld), ("pe", xq_pe[i])],
                             inc="act")
            for jj in range(4):
                xb_pe[i] = p.op(
                    "tensor", "matmul", psA[0:1, jj * 512:jj * 512 + 512],
                    onesc[:], xb[i][:, jj * 512:jj * 512 + 512],
                    start=(c == 0), stop=(c == DC - 1), skip_group_check=True,
                    waits=[(f"xb{i}", ld)] + CONST_G if jj == 0 else [],
                    inc="pe" if jj == 3 else None)
            for jj in range(4):
                xq_pe[i] = p.op(
                    "tensor", "matmul",
                    psA[0:1, 2048 + jj * 512:2048 + jj * 512 + 512],
                    onesc[:], xq[i][:, jj * 512:jj * 512 + 512],
                    start=(c == 0), stop=(c == DC - 1), skip_group_check=True,
                    waits=[("act", xb_act[i])] if jj == 0 else [],
                    inc="pe" if jj == 3 else None)
        stats_pe = p.now("pe")
        mrow = ROWS[0:1, 0:2048]
        vrow = ROWS[0:1, 2048:4096]
        p.op("vector", "tensor_scalar", mrow, psA[0:1, 0:2048], 1.0 / 2048.0,
             None, ALU.mult, waits=[("pe", stats_pe)])
        p.op("vector", "tensor_tensor", vrow, mrow, mrow, ALU.mult)
        rows_dve = p.op("vector", "scalar_tensor_tensor", out=vrow,
                        in0=psA[0:1, 2048:4096], scalar=1.0 / 2048.0,
                        in1=vrow, op0=ALU.mult, op1=ALU.subtract, inc="dve")
        p.op("scalar", "activation", vrow, vrow, AF.Ln, bias=cb_sb[0:1, 1:2],
             waits=[("dve", rows_dve)] + CONST_S)
        rstd_act = p.op("scalar", "activation", vrow, vrow, AF.Exp,
                        scale=-0.5, inc="act")
        for jj in range(4):
            p.op("tensor", "matmul", psA[:, jj * 512:jj * 512 + 512],
                 onesr[:], mrow[0:1, jj * 512:jj * 512 + 512], start=True,
                 stop=True, skip_group_check=True,
                 waits=[("dve", rows_dve)] + CONST_S if jj == 0 else [])
            bc_pe = p.op("tensor", "matmul",
                         psA[:, 2048 + jj * 512:2048 + jj * 512 + 512],
                         onesr[:], vrow[0:1, jj * 512:jj * 512 + 512],
                         start=True, stop=True, skip_group_check=True,
                         waits=[("act", rstd_act)] if jj == 0 else [],
                         inc="pe" if jj == 3 else None)
        m_b = pb(8192, 2048)
        r_b = pb(10240, 2048)
        p.op("vector", "tensor_copy", m_b, psA[:, 0:2048],
             waits=[("pe", bc_pe)])
        bc_dve = p.op("vector", "tensor_copy", r_b, psA[:, 2048:4096],
                      inc="dve")
        tmp = pb(4096, 2048)
        ap_dve = [0, 0]
        last = bc_dve
        for c in range(DC):
            i = c % 2
            ld = p.dma("gpsimd", f"xb{i}", xb[i],
                       xsrc[c * 128:(c + 1) * 128, :],
                       waits=[("pe", xb_pe[i]), ("act", xb_act[i]),
                              ("dve", ap_dve[i])])
            p.op("vector", "tensor_tensor", tmp, xb[i], m_b, ALU.subtract,
                 waits=[(f"xb{i}", ld), ("dve", bc_dve)])
            last = p.op("vector", "tensor_tensor",
                        xln[:, c * 2048:(c + 1) * 2048], tmp, r_b, ALU.mult,
                        inc="dve")
            ap_dve[i] = last
            xb_pe[i] = 0
            xb_act[i] = 0
        return last

    wsl = [pb(12288, 2048), pb(14336, 2048)]
    wsl_rd = [0, 0]

    def proj_head(fidx, wsrc, xln_dve, epilog):
        i = fidx % 2
        lw = p.dma("gpsimd", f"ws{i}", wsl[i], wsrc,
                   waits=[("pe", wsl_rd[i])])
        for blk in range(NB):
            sl = blk % 2
            pp = psA[:, sl * 512:sl * 512 + 512]
            for c in range(DC):
                w = []
                if c == 0:
                    w = [(f"ws{i}", lw), ("dve", xln_dve),
                         ("act", state.get(("ppfree", sl), 0))]
                pe_c = p.op(
                    "tensor", "matmul", pp,
                    wsl[i][:, c * 128:c * 128 + 128],
                    xln[:, c * 2048 + blk * 512:c * 2048 + blk * 512 + 512],
                    start=(c == 0), stop=(c == DC - 1), skip_group_check=True,
                    waits=w, inc="pe" if c == DC - 1 else None)
            if blk >= NB - 2:
                wsl_rd[i] = pe_c
            epilog(blk, pp, pe_c)

    RP = dict(tq_w=[[], []], qo_w=[[], []], pr_w=[], cs_n=0)

    def rope_epilog(layer, dst_of_blk, wstream):
        CS = pb(16384, 4096)
        tq = [pb(20480, 512), pb(20992, 512)]
        qo = [pb(21504, 512), pb(22016, 512)]
        t1 = pb(22528, 512)
        t2 = pb(23040, 512)
        prmv = prm_sb[:, layer * 128:layer * 128 + 128]

        def ep(blk, pp, pe_c):
            i = blk % 2
            ev = p.op("scalar", "copy", tq[i], pp,
                      waits=[("pe", pe_c)] + RP["tq_w"][i], inc="act")
            state[("ppfree", blk % 2)] = ev
            sw = p.op("tensor", "matmul", psA[:, 1024:1536], prmv, tq[i],
                      start=True, stop=True, skip_group_check=True,
                      waits=[("act", ev)] + CONST_G + RP["pr_w"], inc="pe")
            t1c = p.op("vector", "tensor_tensor", t1, tq[i],
                       CS[:, blk * 512:blk * 512 + 512], ALU.mult,
                       waits=[("act", ev), ("cs", RP["cs_n"])], inc="dve")
            t2c = p.op("vector", "tensor_tensor", t2, psA[:, 1024:1536],
                       CS[:, 2048 + blk * 512:2048 + blk * 512 + 512],
                       ALU.mult, waits=[("pe", sw)], inc="dve")
            RP["pr_w"] = [("dve", t2c)]
            qoc = p.op("vector", "tensor_tensor", qo[i], t1, t2, ALU.add,
                       waits=RP["qo_w"][i], inc="dve")
            RP["tq_w"][i] = [("pe", sw), ("dve", t1c)]
            wn = p.dma("sync", f"{wstream}{i}", dst_of_blk(blk), qo[i],
                       waits=[("dve", qoc)])
            RP["qo_w"][i] = [(f"{wstream}{i}", wn)]
        return ep

    def emit_vproj(wv_src, xln_dve):
        vo_sb = pb(4096, 8192)
        pv_dve = [0, 0]
        for u in range(2):
            a, b_ = u % 2, 1 - (u % 2)
            la = p.dma("gpsimd", f"ws{a}", wsl[a], wv_src[u][:, 0:2048],
                       waits=[("pe", wsl_rd[a])])
            lb = p.dma("gpsimd", f"ws{b_}", wsl[b_], wv_src[u][:, 2048:4096],
                       waits=[("pe", wsl_rd[b_])])
            for tt in range(16):
                sl = tt % 2
                pv = psA[:, 1536 + sl * 512:1536 + sl * 512 + 256]
                for c in range(DC):
                    w = [(f"ws{a}", la), (f"ws{b_}", lb), ("dve", xln_dve),
                         ("dve", pv_dve[sl])] if c == 0 else []
                    slab = wsl[a] if c < 8 else wsl[b_]
                    cc = c % 8
                    pe_c = p.op(
                        "tensor", "matmul", pv,
                        xln[:, c * 2048 + tt * 128:
                            c * 2048 + tt * 128 + 128],
                        slab[:, cc * 256:cc * 256 + 256],
                        start=(c == 0), stop=(c == DC - 1),
                        skip_group_check=True, waits=w,
                        inc="pe" if c == DC - 1 else None)
                if tt >= 14:
                    wsl_rd[a] = pe_c
                    wsl_rd[b_] = pe_c
                d = 0
                for jj in range(2):
                    j = 2 * u + jj
                    d = p.op("vector", "tensor_copy",
                             vo_sb[:, j * 2048 + tt * 128:
                                   j * 2048 + tt * 128 + 128],
                             pv[:, jj * 128:jj * 128 + 128],
                             waits=[("pe", pe_c)] if jj == 0 else [],
                             inc="dve" if jj == 1 else None)
                pv_dve[sl] = d
        dv = p.now("dve")
        for j in range(4):
            p.dma("sync", "vw", vown.ap()[j * 128:(j + 1) * 128, :],
                  vo_sb[:, j * 2048:(j + 1) * 2048],
                  waits=[("dve", dv)] if j == 0 else [])
        return p.full("vw")

    def emit_core(layer):
        ck, cv = ("cka", "cva") if layer == 0 else ("ckb", "cvb")
        Kb = [pb(0, 4096), pb(8192, 4096)]
        Vb = [pb(4096, 4096), pb(12288, 4096)]
        qb = [pb(16384, 1024), pb(17408, 1024)]
        eb = [pb(18432, 1024), pb(19456, 1024)]
        ob = [pb(20480, 1024), pb(21504, 1024)]
        ou = [pb(22528, 1024), pb(23552, 1024)]
        ou_rd = [0, 0]
        rb = pf(2048, 1024)
        st = [psA[:, 0:1024], psA[:, 1024:2048]]
        av = psA[:, 2048:3072]
        sm = psA[0:1, 3072:4096]
        kv_n = [0, 0]
        kv_rd = [0, 0]
        q_n = [0, 0]
        q_rd = [0, 0]
        ow_n = [0, 0]
        exp_done = [0, 0]
        av_rd = [0, 0]
        acc_n = [0, 0]
        sm_rd = [0]
        if layer == 0:
            accs = [GS[:, 0:512], GS[:, 512:1024]]
        else:
            acc = GS[:, 0:1024]
        ki = qi = 0
        rbev = 0
        av_free = 0
        sm_free = 0
        qd_done = state["qd_done"]

        def load_kv(j, slot, first=False):
            w = [("pe", kv_rd[slot]), (ck, p.now(ck)), (cv, p.now(cv))]
            if first:
                w += qd_done
            p.dma("sync", f"kv{slot}", Kb[slot][:, 0:2048],
                  kg.ap()[j * 128:(j + 1) * 128, :], waits=w)
            p.dma("sync", f"kv{slot}", Kb[slot][:, 2048:4096],
                  kg.ap()[512 + j * 128:512 + j * 128 + 128, :])
            p.dma("sync", f"kv{slot}", Vb[slot][:, 0:2048],
                  vg.ap()[j * 128:(j + 1) * 128, :])
            p.dma("sync", f"kv{slot}", Vb[slot][:, 2048:4096],
                  vg.ap()[512 + j * 128:512 + j * 128 + 128, :])
            return p.now(f"kv{slot}")

        kv_n[0] = load_kv(0, 0, first=True)
        for g in range(2):
            for j in range(NKV):
                slot = (g * NKV + j) % 2
                for qh in range(4):
                    h = j * 4 + qh
                    qsl = qi % 2
                    q_n[qsl] = p.dma(
                        "sync", f"ql{qsl}", qb[qsl],
                        Qd[h * 128:(h + 1) * 128, g * 1024:g * 1024 + 1024],
                        waits=[("pe", q_rd[qsl])] + qd_done)
                    if qh == 0 and (g, j) != (1, NKV - 1):
                        nj, ng = (j + 1, g) if j < NKV - 1 else (0, g + 1)
                        kv_n[1 - slot] = load_kv(nj, 1 - slot)
                    if layer == 0:
                        tiles = []
                        for h01 in range(2):
                            n = len(TILES[g][h01])
                            for idx, (t, msk) in enumerate(TILES[g][h01]):
                                tiles.append((h01, t, msk, idx == n - 1))
                    else:
                        tiles = [(None, t, False, t == 31)
                                 for t in range(32)]
                    pend = None
                    first = {0: True, 1: True, None: True}

                    def flush(pd):
                        nonlocal av_free, sm_free
                        sp, h01, t, gate, fa, la = pd
                        w = [gate]
                        if fa:
                            w += [("act", av_free), ("act", sm_free)]
                        if layer == 0:
                            off = h01 * 512
                            av_rd[sp] = p.op(
                                "tensor", "matmul", av[:, off:off + 512],
                                Vb[slot][:, t * 128:t * 128 + 128],
                                eb[sp][:, 0:512], start=fa, stop=la,
                                skip_group_check=True, waits=w, inc="pe")
                            if fa:
                                acc_n[sp] = p.op(
                                    "vector", "tensor_copy", accs[h01],
                                    eb[sp][:, 0:512],
                                    waits=[gate, ("pe", sm_rd[0])],
                                    inc="dve")
                            else:
                                acc_n[sp] = p.op(
                                    "vector", "tensor_tensor", accs[h01],
                                    accs[h01], eb[sp][:, 0:512], ALU.add,
                                    waits=[gate], inc="dve")
                        else:
                            for jj in range(2):
                                av_rd[sp] = p.op(
                                    "tensor", "matmul",
                                    av[:, jj * 512:jj * 512 + 512],
                                    Vb[slot][:, t * 128:t * 128 + 128],
                                    eb[sp][:, jj * 512:jj * 512 + 512],
                                    start=fa, stop=la, skip_group_check=True,
                                    waits=w if jj == 0 else [],
                                    inc="pe" if jj == 1 else None)
                            if fa:
                                acc_n[sp] = p.op(
                                    "vector", "tensor_copy", acc,
                                    eb[sp][:, 0:1024],
                                    waits=[gate, ("pe", sm_rd[0])],
                                    inc="dve")
                            else:
                                acc_n[sp] = p.op(
                                    "vector", "tensor_tensor", acc, acc,
                                    eb[sp][:, 0:1024], ALU.add,
                                    waits=[gate], inc="dve")

                    for (h01, t, msk, is_last) in tiles:
                        sp = ki % 2
                        ww = [("act", exp_done[sp]), (f"ql{qsl}", q_n[qsl]),
                              (f"kv{slot}", kv_n[slot]), ("act", rbev)]
                        if layer == 0:
                            qk = p.op("tensor", "matmul", st[sp][:, 0:512],
                                      Kb[slot][:, t * 128:t * 128 + 128],
                                      qb[qsl][:, h01 * 512:h01 * 512 + 512],
                                      start=True, stop=True,
                                      skip_group_check=True, waits=ww,
                                      inc="pe")
                            width = 512
                        else:
                            p.op("tensor", "matmul", st[sp][:, 0:512],
                                 Kb[slot][:, t * 128:t * 128 + 128],
                                 qb[qsl][:, 0:512], start=True, stop=True,
                                 skip_group_check=True, waits=ww)
                            qk = p.op("tensor", "matmul", st[sp][:, 512:1024],
                                      Kb[slot][:, t * 128:t * 128 + 128],
                                      qb[qsl][:, 512:1024], start=True,
                                      stop=True, skip_group_check=True,
                                      inc="pe")
                            width = 1024
                        if pend is not None:
                            flush(pend)
                        ex = p.op("scalar", "activation", eb[sp][:, 0:width],
                                  st[sp][:, 0:width], AF.Exp,
                                  bias=cb_sb[:, 0:1], scale=SCALE,
                                  waits=[("pe", qk), ("pe", av_rd[sp]),
                                         ("dve", acc_n[sp])]
                                  + CONST_S, inc="act")
                        exp_done[sp] = ex
                        gate = ("act", ex)
                        if layer == 0 and msk:
                            blkcol = (2 * g + h01) * 512
                            mk = p.op("vector", "scalar_tensor_tensor",
                                      out=eb[sp][:, 0:512],
                                      in0=qidc_sb[:, blkcol:blkcol + 512],
                                      scalar=kid_sb[:, t:t + 1],
                                      in1=eb[sp][:, 0:512],
                                      op0=ALU.is_ge, op1=ALU.mult,
                                      waits=[("act", ex)], inc="dve")
                            gate = ("dve", mk)
                        pend = (sp, h01, t, gate, first[h01], is_last)
                        first[h01] = False
                        ki += 1
                    flush(pend)
                    accq = p.now("dve")
                    for jj in range(2):
                        smw = [("dve", accq), ("act", sm_free)] \
                            if jj == 0 else []
                        if layer == 0:
                            smsrc = accs[jj]
                        else:
                            smsrc = acc[:, jj * 512:jj * 512 + 512]
                        smlhs = onesc[:]
                        p.op("tensor", "matmul",
                             sm[0:1, jj * 512:jj * 512 + 512], smlhs, smsrc,
                             start=True, stop=True, skip_group_check=True,
                             waits=smw, inc="pe" if jj == 1 else None)
                    last_pe = p.now("pe")
                    sm_rd[0] = last_pe
                    q_rd[qsl] = last_pe
                    kv_rd[slot] = last_pe
                    p.op("scalar", "activation", ROWS[0:1, 0:1024], sm,
                         AF.Ln, waits=[("pe", last_pe)])
                    recip = p.op("scalar", "activation", ROWS[0:1, 0:1024],
                                 ROWS[0:1, 0:1024], AF.Exp, scale=-1.0,
                                 inc="act")
                    sm_free = recip
                    for jj in range(2):
                        bc_pe = p.op(
                            "tensor", "matmul",
                            st[0][:, jj * 512:jj * 512 + 512], onesr[:],
                            ROWS[0:1, jj * 512:jj * 512 + 512], start=True,
                            stop=True, skip_group_check=True,
                            waits=[("act", recip), ("act", exp_done[0]),
                                   ("act", exp_done[1])] if jj == 0 else [],
                            inc="pe" if jj == 1 else None)
                    rbev = p.op("scalar", "copy", rb, st[0][:, 0:1024],
                                waits=[("pe", bc_pe)], inc="act")
                    exp_done[0] = rbev
                    osl = qi % 2
                    oun = p.op("scalar", "copy", ou[osl], av,
                               waits=[("pe", last_pe),
                                      ("dve", ou_rd[osl])], inc="act")
                    av_free = oun
                    osc = p.op("vector", "tensor_tensor", ob[osl], ou[osl],
                               rb, ALU.mult,
                               waits=[("act", rbev), ("act", oun),
                                      (f"ow{osl}", ow_n[osl])], inc="dve")
                    ou_rd[osl] = osc
                    ow_n[osl] = p.dma(
                        "sync", f"ow{osl}",
                        Od[h * 128:(h + 1) * 128, g * 1024:g * 1024 + 1024],
                        ob[osl], waits=[("dve", osc)])
                    qi += 1
                if g == 0:
                    state["od_g0"] = p.full("ow0", "ow1")
        state["od_done"] = p.full("ow0", "ow1")

    def emit_wo(layer, wsrc, xsrc, xdst, gate):
        oblk = pb(0, 16384)
        wslo = [pb(16384, 2048), pb(18432, 2048)]
        wslo_rd = [0, 0]
        ysl = [pb(20480, 512), pb(20992, 512)]
        yw_n = [0, 0]
        xo = [pf(2048, 512), pf(2560, 512)]
        wofree = [0, 0]
        od_done = state["od_done"]
        od_g0 = state["od_g0"]
        oblk_rd = 0
        for tp in range(2):
            for c in range(DC):
                w = ((od_g0 if tp == 0 else od_done)
                     + [("pe", oblk_rd)]) if c == 0 else []
                p.dma("sync", "obl", oblk[:, c * 1024:(c + 1) * 1024],
                      Od[c * 128:(c + 1) * 128, tp * 1024:tp * 1024 + 1024],
                      waits=w)
            oq = p.now("obl")
            for f in range(DC):
                i = f % 2
                w = [("pe", wslo_rd[i])]
                if tp == 0 and f == 0:
                    w += od_done
                lw = p.dma("gpsimd", f"wo{i}", wslo[i], wsrc[f], waits=w)

                for tb in range(2):
                    blk = tp * 2 + tb
                    sl = blk % 2
                    pp = psA[:, sl * 512:sl * 512 + 512]
                    for c in range(DC):
                        w = [(f"wo{i}", lw), ("obl", oq),
                             (("act", wofree[sl]) if layer == 0
                              else ("dve", wofree[sl]))] if c == 0 else []
                        pe_c = p.op(
                            "tensor", "matmul", pp,
                            wslo[i][:, c * 128:c * 128 + 128],
                            oblk[:, c * 1024 + tb * 512:
                                 c * 1024 + tb * 512 + 512],
                            start=(c == 0), stop=(c == DC - 1),
                            skip_group_check=True, waits=w,
                            inc="pe" if c == DC - 1 else None)
                    if tb == 1:
                        wslo_rd[i] = pe_c
                    if layer == 0:
                        ev = p.op("scalar", "copy", ysl[sl], pp,
                                  waits=[("pe", pe_c),
                                         (f"yw{sl}", yw_n[sl])], inc="act")
                        wofree[sl] = ev
                        yw_n[sl] = p.dma(
                            "sync", f"yw{sl}",
                            Yd[f * 128:(f + 1) * 128,
                               blk * 512:blk * 512 + 512], ysl[sl],
                            waits=[("act", ev)])
                    else:
                        w = [(f"xw{sl}", state[("xw", sl)])]
                        if tp == 0 and f == 0 and tb == 0:
                            w += gate
                        xq = p.dma("sync", f"xl{sl}", xo[sl],
                                   xsrc[f * 128:(f + 1) * 128,
                                        blk * 512:blk * 512 + 512], waits=w)
                        ad = p.op("vector", "tensor_tensor", xo[sl], xo[sl],
                                  pp, ALU.add,
                                  waits=[("pe", pe_c), (f"xl{sl}", xq)],
                                  inc="dve")
                        wofree[sl] = ad
                        state[("xw", sl)] = p.dma(
                            "sync", f"xw{sl}",
                            xdst[f * 128:(f + 1) * 128,
                                 blk * 512:blk * 512 + 512], xo[sl],
                            waits=[("dve", ad)])
            oblk_rd = p.now("pe")
            if tp == 0:
                state["yd_tp0"] = p.full("yw0", "yw1")
        return p.full("xw0", "xw1", "yw0", "yw1")

    def emit_rms(xsrc, xdst, gate):
        yblk = [pb(0, 8192), pb(10240, 8192)]
        ysq = [pb(8192, 512), pb(8704, 512)]
        rrb = pb(9216, 512)
        t3 = pb(9728, 512)
        xo = [pf(2048, 512), pf(2560, 512)]
        ysq_pe = [0, 0]
        ybq = [0, 0]
        yb_act = [0, 0]
        yb_dve = [0, 0]
        yd_tp0 = state["yd_tp0"]
        yd_done = state["yd_done"]
        rr_dve = 0
        streams = ["ybl", "yb2"]

        def load_yblk(tb, i):
            w = [("act", yb_act[i]), ("dve", yb_dve[i])]
            w += yd_tp0 if tb < 2 else yd_done
            for c in range(DC):
                p.dma("sync", streams[i], yblk[i][:, c * 512:(c + 1) * 512],
                      Yd[c * 128:(c + 1) * 128, tb * 512:tb * 512 + 512],
                      waits=w if c == 0 else [])
            return p.now(streams[i])

        ybq[0] = load_yblk(0, 0)
        for tb in range(NB):
            i = tb % 2
            if tb < NB - 1:
                ybq[1 - i] = load_yblk(tb + 1, 1 - i)
            for c in range(DC):
                ii = c % 2
                sq = p.op("scalar", "activation", ysq[ii],
                          yblk[i][:, c * 512:(c + 1) * 512], AF.Square,
                          waits=[(streams[i], ybq[i]), ("pe", ysq_pe[ii])],
                          inc="act")
                ysq_pe[ii] = p.op(
                    "tensor", "matmul", psA[0:1, 1024:1536], onesc[:],
                    ysq[ii], start=(c == 0), stop=(c == DC - 1),
                    skip_group_check=True,
                    waits=[("act", sq)] + ([("dve", rr_dve)] if c == 0
                                           else []), inc="pe")
            spe = p.now("pe")
            yb_act[i] = p.now("act")
            mr = ROWS[0:1, 0:512]
            lr = ROWS[0:1, 512:1024]
            d0 = p.op("vector", "tensor_scalar", mr, psA[0:1, 1024:1536],
                      1.0 / 2048.0, None, ALU.mult, waits=[("pe", spe)],
                      inc="dve")
            p.op("scalar", "activation", lr, mr, AF.Ln, bias=cb_sb[0:1, 2:3],
                 waits=[("dve", d0)])
            ra = p.op("scalar", "activation", mr, lr, AF.Exp, scale=-0.5,
                      inc="act")
            rpe = p.op("tensor", "matmul", psA[:, 1536:2048], onesr[:], mr,
                       start=True, stop=True, skip_group_check=True,
                       waits=[("act", ra), ("dve", d0)], inc="pe")
            rr_dve = p.op("vector", "tensor_copy", rrb, psA[:, 1536:2048],
                          waits=[("pe", rpe)], inc="dve")
            for c in range(DC):
                sl = c % 2
                w = [(f"xw{sl}", state[("xw", sl)])]
                if tb == 0 and c == 0:
                    w += gate
                xq = p.dma("sync", f"xl{sl}", xo[sl],
                           xsrc[c * 128:(c + 1) * 128,
                                tb * 512:tb * 512 + 512], waits=w)
                p.op("vector", "scalar_tensor_tensor", out=t3,
                     in0=yblk[i][:, c * 512:(c + 1) * 512],
                     scalar=nwa_sb[:, c:c + 1], in1=rrb,
                     op0=ALU.mult, op1=ALU.mult,
                     waits=[("dve", rr_dve)] + CONST_S)
                ad = p.op("vector", "tensor_tensor", xo[sl], xo[sl], t3,
                          ALU.add, waits=[(f"xl{sl}", xq)], inc="dve")
                state[("xw", sl)] = p.dma(
                    "sync", f"xw{sl}",
                    xdst[c * 128:(c + 1) * 128, tb * 512:tb * 512 + 512],
                    xo[sl], waits=[("dve", ad)])
            yb_dve[i] = p.now("dve")
        return p.full("xw0", "xw1")

    def emit_ffn(xln_dve, ffn_idx, xsrc, xdst, gate):
        ccs = "cf1" if ffn_idx == 0 else "cf2"
        lg = [psA[:, 0:12], psA[:, 512:524]]
        tp = [psA[0:12, 1024:1152], psA[0:12, 1536:1664]]
        Es = sel[:, 0:12]
        O1 = sel[:, 12:24]
        O2 = sel[:, 24:36]
        csA = sel[:, 36:48]
        csB = sel[:, 48:60]
        E2 = sel[:, 60:72]
        inv = sel[:, 72:84]
        m1 = rw_sb[:, 0:1]
        m1n = rw_sb[:, 1:2]
        m2e = rw_sb[:, 2:3]
        den = rw_sb[:, 3:4]
        rw1 = rw_sb[:, 4:5]
        rw2 = rw_sb[:, 5:6]
        lg_free = [(0, 0), (0, 0)]
        tp_free = [0, 0]
        for tt in range(16):
            i = tt % 2
            for c in range(DC):
                w = [("dve", xln_dve), ("dve", lg_free[i][0]),
                     ("act", lg_free[i][1])] + CONST_G if c == 0 else []
                lpe = p.op(
                    "tensor", "matmul", lg[i],
                    xln[:, c * 2048 + tt * 128:c * 2048 + tt * 128 + 128],
                    gw_sb[:, c * 12:c * 12 + 12], start=(c == 0),
                    stop=(c == DC - 1), skip_group_check=True, waits=w,
                    inc="pe" if c == DC - 1 else None)
            p.op("vector", "tensor_reduce", m1, lg[i], AX.X, ALU.max,
                 waits=[("pe", lpe)])
            p.op("vector", "tensor_scalar", m1n, m1, -1.0, None, ALU.mult)
            d = p.op("vector", "tensor_scalar", O1, lg[i], m1, None,
                     ALU.is_ge, inc="dve")
            sa = p.op("scalar", "activation", Es, lg[i], AF.Exp, bias=m1n,
                      waits=[("dve", d)], inc="act")
            p.op("vector", "tensor_tensor_scan", out=csA, data0=O1,
                 data1=O1, initial=0.0, op0=ALU.add, op1=ALU.max)
            p.op("vector", "tensor_tensor", csB, csA, O1, ALU.subtract)
            p.op("vector", "tensor_scalar", csB, csB, 0.5, None, ALU.is_lt)
            p.op("vector", "tensor_tensor", O1, O1, csB, ALU.mult)
            p.op("vector", "tensor_scalar", inv, O1, -1.0, 1.0, ALU.mult,
                 ALU.add)
            p.op("vector", "tensor_tensor", E2, Es, inv, ALU.mult,
                 waits=[("act", sa)])
            p.op("vector", "tensor_reduce", m2e, E2, AX.X, ALU.max)
            p.op("vector", "tensor_scalar", O2, E2, m2e, None, ALU.is_ge)
            p.op("vector", "tensor_tensor_scan", out=csA, data0=O2,
                 data1=O2, initial=0.0, op0=ALU.add, op1=ALU.max)
            p.op("vector", "tensor_tensor", csB, csA, O2, ALU.subtract)
            p.op("vector", "tensor_scalar", csB, csB, 0.5, None, ALU.is_lt)
            p.op("vector", "tensor_tensor", O2, O2, csB, ALU.mult)
            p.op("vector", "tensor_scalar", den, m2e, 1.0, None, ALU.add)
            p.op("vector", "reciprocal", rw1, den)
            p.op("vector", "tensor_scalar", rw2, rw1, -1.0, 1.0, ALU.mult,
                 ALU.add)
            p.op("vector", "tensor_copy", OS1[:, tt * 12:tt * 12 + 12], O1)
            p.op("vector", "tensor_copy", OS2[:, tt * 12:tt * 12 + 12], O2)
            p.op("vector", "tensor_copy", RWS[:, 2 * tt:2 * tt + 1], rw1)
            d1 = p.op("vector", "tensor_copy", RWS[:, 2 * tt + 1:2 * tt + 2],
                      rw2, inc="dve")
            lg_free[i] = (d1, sa)
            tr1 = p.op("tensor", "transpose", tp[i][0:12, 0:128], O1,
                       id_sb[:], waits=[("dve", d1), ("dve", tp_free[i])]
                       + CONST_S, inc="pe")
            e1 = p.op("vector", "tensor_copy",
                      RT1[:, tt * 128:tt * 128 + 128], tp[i][0:12, :],
                      waits=[("pe", tr1)], inc="dve")
            tr2 = p.op("tensor", "transpose", tp[i][0:12, 0:128], O2,
                       id_sb[:], waits=[("dve", e1)], inc="pe")
            tp_free[i] = p.op("vector", "tensor_copy",
                              RT2[:, tt * 128:tt * 128 + 128],
                              tp[i][0:12, :], waits=[("pe", tr2)], inc="dve")
        sel_done = p.now("dve")
        for bpos in range(4):
            for s in range(2):
                RT = RT1 if s == 0 else RT2
                u = bpos * 2 + s
                cd = p.op("vector", "tensor_reduce", cnts_sb[:, u:u + 1],
                          RT[:, bpos * 512:bpos * 512 + 512], AX.X, ALU.add,
                          inc="dve" if (bpos == 3 and s == 1) else None)
        cq = p.dma("sync", "cw",
                   contrib.ap().rearrange("o (u e) -> (o e) u", u=8, e=12),
                   cnts_sb[:], waits=[("dve", cd)],
                   allow_slow_non_contiguous=True)
        p.op("gpsimd", "collective_compute", "AllGather", ALU.bypass,
             replica_groups=[list(range(8))],
             ins=[contrib.ap().opt()], outs=[gath.ap().opt()],
             waits=[("cw", cq)], inc=ccs)
        M = PF[0:12, 0:2048]
        m_dve = [0]

        def emit_mphase():
            gq = p.dma("sync", "gl", g64_sb[:],
                       gath.ap().rearrange("r (u e) -> (r u) e", u=8, e=12),
                       waits=[(ccs, p.now(ccs))])
            bmm = p.op("tensor", "matmul", psA[0:12, 2048:2056], g64_sb[:],
                       prec_sb[:], start=True, stop=True,
                       skip_group_check=True,
                       waits=[("gl", gq)] + CONST_S, inc="pe")
            thr_dve = p.op("vector", "tensor_scalar", thr_sb[:],
                           psA[0:12, 2048:2056], -1.0, float(CAP) + 0.5,
                           ALU.mult, ALU.add, waits=[("pe", bmm)], inc="dve")
            kp1 = psA[:, 1024:1036]
            kp2 = psA[:, 1536:1548]
            mp = psA[0:12, 2048:2176]
            ktm1 = sel[:, 36:48]
            ktm2 = sel[:, 48:60]
            tA = sel[:, 60:72]
            tB = sel[:, 72:84]
            kp_dve = [0, 0]
            mp_dve = 0
            for bpos in range(4):
                p.op("vector", "tensor_tensor_scan", out=rs_sb[:],
                     data0=RT1[:, bpos * 512:bpos * 512 + 512],
                     data1=RT1[:, bpos * 512:bpos * 512 + 512],
                     initial=0.0, op0=ALU.add, op1=ALU.max,
                     waits=[("dve", thr_dve)])
                p.op("vector", "tensor_scalar", rs_sb[:], rs_sb[:],
                     thr_sb[:, 2 * bpos:2 * bpos + 1], None, ALU.is_lt)
                p.op("vector", "tensor_tensor_scan", out=rs2_sb[:],
                     data0=RT2[:, bpos * 512:bpos * 512 + 512],
                     data1=RT2[:, bpos * 512:bpos * 512 + 512],
                     initial=0.0, op0=ALU.add, op1=ALU.max)
                kd = p.op("vector", "tensor_scalar", rs2_sb[:], rs2_sb[:],
                          thr_sb[:, 2 * bpos + 1:2 * bpos + 2], None,
                          ALU.is_lt, inc="dve")
                for q in range(4):
                    tt = bpos * 4 + q
                    tk1 = p.op("tensor", "transpose", kp1,
                               rs_sb[:, q * 128:q * 128 + 128],
                               id_sb[0:12, 0:12],
                               waits=[("dve", kd), ("dve", kp_dve[0])],
                               inc="pe")
                    tk2 = p.op("tensor", "transpose", kp2,
                               rs2_sb[:, q * 128:q * 128 + 128],
                               id_sb[0:12, 0:12],
                               waits=[("dve", kp_dve[1])], inc="pe")
                    p.op("vector", "tensor_copy", ktm1, kp1,
                         waits=[("pe", tk1)])
                    kp_dve[0] = p.op("vector", "tensor_copy", ktm2, kp2,
                                     waits=[("pe", tk2)], inc="dve")
                    kp_dve[1] = kp_dve[0]
                    p.op("vector", "tensor_tensor", tA, ktm1,
                         OS1[:, tt * 12:tt * 12 + 12], ALU.mult)
                    p.op("vector", "tensor_scalar", tA, tA,
                         RWS[:, 2 * tt:2 * tt + 1], None, ALU.mult)
                    p.op("vector", "tensor_tensor", tB, ktm2,
                         OS2[:, tt * 12:tt * 12 + 12], ALU.mult)
                    p.op("vector", "tensor_scalar", tB, tB,
                         RWS[:, 2 * tt + 1:2 * tt + 2], None, ALU.mult)
                    md = p.op("vector", "tensor_tensor", tA, tA, tB, ALU.add,
                              inc="dve")
                    tm = p.op("tensor", "transpose", mp, tA, id_sb[:],
                              waits=[("dve", md), ("dve", mp_dve)], inc="pe")
                    mp_dve = p.op("vector", "tensor_copy",
                                  M[:, tt * 128:tt * 128 + 128], mp,
                                  waits=[("pe", tm)], inc="dve")
            m_dve[0] = mp_dve

        w13s = [pb(0, 4096), pb(4096, 4096), pb(8192, 4096), pb(12288, 4096)]
        w13_n = [0, 0, 0, 0]
        w13_rd = [0, 0, 0, 0]
        s1 = pb(16384, 1024)
        g1 = pb(17408, 1024)
        mbf = pf(2048, 1024)
        w2sl = [pb(18432, 3072), pb(21504, 3072)]
        w2_nl = [0, 0]
        w2_rdl = [0, 0]
        h1 = psA[:, 0:1024]
        h3 = psA[:, 1024:2048]
        mb = psA[:, 2048:3072]
        mrow_ps = psA[0:1, 3072:4096]
        h_free = 0
        mb_free = 0
        mrow_rd = 0
        gs_rd = 0
        py_free = [0, 0]
        first_h = True
        for g in range(2):
            for e in range(NE):
                sA = 2 * (e % 2)
                w = [("pe", w13_rd[sA])]
                if first_h:
                    w += [("dve", xln_dve), ("dve", sel_done)]
                w13_n[sA] = p.dma("gpsimd", "w130" if sA == 0 else "w131",
                                  pb(sA * 4096, 8192),
                                  w13[e], waits=w + [("pe", w13_rd[sA + 1])])
                w13_n[sA + 1] = w13_n[sA]
                for fc in range(2):
                    for half in range(2):
                        for c in range(DC):
                            w = []
                            if half == 0 and c == 0:
                                w = [("act", h_free)]
                                if fc == 0:
                                    w += [("w130" if sA == 0 else "w131",
                                           w13_n[sA]),
                                          ("dve", xln_dve)]
                                    if first_h:
                                        w += [("dve", sel_done)]
                                    prev = state.get(("gsrd", e), 0)
                                    if prev:
                                        w.append(("pe", prev))
                            p.op("tensor", "matmul",
                                 h1[:, half * 512:half * 512 + 512],
                                 w13s[sA][:, (fc * DC + c) * 128:
                                          (fc * DC + c) * 128 + 128],
                                 xln[:, c * 2048 + g * 1024 + half * 512:
                                     c * 2048 + g * 1024 + half * 512 + 512],
                                 start=(c == 0), stop=(c == DC - 1),
                                 skip_group_check=True, waits=w)
                    for half in range(2):
                        for c in range(DC):
                            hpe = p.op(
                                "tensor", "matmul",
                                h3[:, half * 512:half * 512 + 512],
                                w13s[sA + 1][:, (fc * DC + c) * 128:
                                             (fc * DC + c) * 128 + 128],
                                xln[:, c * 2048 + g * 1024 + half * 512:
                                    c * 2048 + g * 1024 + half * 512 + 512],
                                start=(c == 0), stop=(c == DC - 1),
                                skip_group_check=True,
                                inc="pe" if (half == 1 and c == DC - 1)
                                else None)
                    if fc == 1:
                        w13_rd[sA] = hpe
                        w13_rd[sA + 1] = hpe
                    ev1 = p.op("scalar", "copy", s1, h1,
                               waits=[("pe", hpe), ("dve", gs_rd)],
                               inc="act")
                    p1 = p.op("vector", "tensor_tensor", g1, s1, h3,
                              ALU.mult, waits=[("act", ev1)], inc="dve")
                    sact = p.op("scalar", "activation", s1, g1, AF.Silu,
                                waits=[("dve", p1)], inc="act")
                    gd = p.op("vector", "tensor_copy",
                              GS[:, (e * 2 + fc) * 1024:
                                 (e * 2 + fc) * 1024 + 1024], s1,
                              waits=[("act", sact)], inc="dve")
                    gs_rd = gd
                    h_free = p.op("scalar", "copy", rw_sb[:, 6:7],
                                  cb_sb[:, 3:4], waits=[("dve", gd)],
                                  inc="act")
                first_h = False
            h_all = p.now("dve")
            if g == 0:
                emit_mphase()
            gate_d = p.now("dve")
            gate_x = p.full("xw0", "xw1")
            for e in range(NE):
                for jj in range(2):
                    w = []
                    if jj == 0:
                        w = [("dve", m_dve[0]), ("dve", mrow_rd),
                             ("dve", gate_d)]
                    mpe = p.op("tensor", "matmul",
                               mrow_ps[0:1, jj * 512:jj * 512 + 512],
                               oh_sb[:, e:e + 1],
                               M[:, g * 1024 + jj * 512:
                                 g * 1024 + jj * 512 + 512],
                               start=True, stop=True, skip_group_check=True,
                               waits=w)
                mrow_rd = p.op("vector", "tensor_copy", ROWS[0:1, 1024:2048],
                               mrow_ps, waits=[("pe", mpe)], inc="dve")
                for jj in range(2):
                    w = [("dve", mrow_rd), ("dve", mb_free)] if jj == 0 else []
                    mbpe = p.op(
                        "tensor", "matmul", mb[:, jj * 512:jj * 512 + 512],
                        onesr[:],
                        ROWS[0:1, 1024 + jj * 512:1024 + jj * 512 + 512],
                        start=True, stop=True, skip_group_check=True,
                        waits=w, inc="pe" if jj == 1 else None)
                mbd = p.op("vector", "tensor_copy", mbf, mb,
                           waits=[("pe", mbpe)] + gate_x, inc="dve")
                mb_free = mbd
                for fc in range(2):
                    sc = p.op("vector", "tensor_tensor",
                              GS[:, (e * 2 + fc) * 1024:
                                 (e * 2 + fc) * 1024 + 1024],
                              GS[:, (e * 2 + fc) * 1024:
                                 (e * 2 + fc) * 1024 + 1024], mbf,
                              ALU.mult, waits=[("dve", mbd)], inc="dve")
            gs_scaled = p.now("dve")
            for f in range(16):
                iw = f % 2
                w = [("pe", w2_rdl[iw])]
                if f == 0:
                    w.append(("dve", gs_scaled))
                w2_nl[iw] = p.dma("gpsimd", "w132" if iw == 0 else "w133",
                                  w2sl[iw], w2p[f], waits=w)
                if True:
                    for blk in range(2):
                        tb = g * 2 + blk
                        sl = blk % 2
                        py = psA[:, 3072 + sl * 512:3072 + sl * 512 + 512]
                        for ef in range(2 * NE):
                            e, fc = ef // 2, ef % 2
                            w = []
                            if ef == 0:
                                w = [("w132" if iw == 0 else "w133",
                                      w2_nl[iw]), ("dve", py_free[sl]),
                                     ("dve", mrow_rd),
                                     ("dve", gs_scaled)]
                            ype = p.op(
                                "tensor", "matmul", py,
                                w2sl[iw][:, ef * 128:ef * 128 + 128],
                                GS[:, ef * 1024 + blk * 512:
                                   ef * 1024 + blk * 512 + 512],
                                start=(ef == 0),
                                stop=(ef == 2 * NE - 1),
                                skip_group_check=True, waits=w,
                                inc="pe" if ef == 2 * NE - 1
                                else None)
                        i2 = (f * 2 + blk) % 2
                        xof = pf(2048 + i2 * 512, 512)
                        xw = [(f"xw{i2}", state[("xw", i2)])]
                        if f == 0 and blk == 0:
                            xw.append(("dve", gs_scaled))
                            if g == 0:
                                xw += gate
                        xq = p.dma("sync", f"xl{i2}", xof,
                                   xsrc[f * 128:(f + 1) * 128,
                                        tb * 512:tb * 512 + 512], waits=xw)
                        ad = p.op("vector", "tensor_tensor", xof, xof, py,
                                  ALU.add, waits=[("pe", ype),
                                                  (f"xl{i2}", xq)],
                                  inc="dve")
                        py_free[sl] = ad
                        state[("xw", i2)] = p.dma(
                            "sync", f"xw{i2}",
                            xdst[f * 128:(f + 1) * 128,
                                 tb * 512:tb * 512 + 512], xof,
                            waits=[("dve", ad)])
                    w2_rdl[iw] = p.now("pe")
            for e in range(NE):
                state[("gsrd", e)] = p.now("pe")
        return p.full("xw0", "xw1")

    def emit_attn(layer, xsrc, xdst, gate):
        xd = emit_ln(xsrc, gate)
        wsl_rd[0] = wsl_rd[1] = p.now("pe")
        cs_src = CSa if layer == 0 else CSb
        CS = pb(16384, 4096)
        p.dma("gpsimd", "cs", CS[:, 0:2048], cs_src[0], waits=gate)
        p.dma("gpsimd", "cs", CS[:, 2048:4096], cs_src[1])
        RP["cs_n"] = p.now("cs")
        RP["pr_w"] = []
        RP["tq_w"] = [[], []]
        RP["qo_w"] = [[], []]
        wk_src = wak if layer == 0 else wbk
        wq_src = waq if layer == 0 else wbq
        wv_src = wav if layer == 0 else wbv
        wo_src = wpa if layer == 0 else wbo

        def kdst(j):
            def f(blk):
                return kown.ap()[j * 128:(j + 1) * 128,
                                 blk * 512:blk * 512 + 512]
            return f

        def qdst(h):
            def f(blk):
                return Qd[h * 128:(h + 1) * 128, blk * 512:blk * 512 + 512]
            return f

        for j in range(NKV):
            proj_head(j, wk_src[j], xd, rope_epilog(layer, kdst(j), "kw"))
        kq = p.full("kw0", "kw1")
        vq = emit_vproj(wv_src, xd)
        ck, cv = ("cka", "cva") if layer == 0 else ("ckb", "cvb")
        p.op("gpsimd", "collective_compute", "AllGather", ALU.bypass,
             replica_groups=[[0, 1], [2, 3], [4, 5], [6, 7]],
             ins=[kown.ap().opt()], outs=[kg.ap().opt()],
             waits=kq + vq, inc=ck)
        p.op("gpsimd", "collective_compute", "AllGather", ALU.bypass,
             replica_groups=[[0, 1], [2, 3], [4, 5], [6, 7]],
             ins=[vown.ap().opt()], outs=[vg.ap().opt()], inc=cv)
        for h in range(NH):
            proj_head(h, wq_src[h], xd, rope_epilog(layer, qdst(h), "qw"))
        state["qd_done"] = p.full("qw0", "qw1")
        emit_core(layer)
        if layer == 0:
            emit_wo(0, wo_src, None, None, gate)
            state["yd_done"] = p.full("yw0", "yw1")
            return emit_rms(xsrc, xdst, gate)
        else:
            return emit_wo(1, wo_src, xsrc, xdst, gate)

    end = emit_attn(0, x_fm, Xd, CONST_S + CONST_G)
    xd = emit_ln(Xd, end)
    end = emit_ffn(xd, 0, Xd, Xd, end)
    end = emit_attn(1, Xd, Xd, end)
    xd = emit_ln(Xd, end)
    end = emit_ffn(xd, 1, Xd, y_fm, end)
    p.op("sync", "nop", waits=[(s, p.now(s)) for s in SEMS])

    sems = {}
    with ExitStack() as sst:
        for s in SEMS:
            sems[s] = sst.enter_context(nc.semaphore(s))
        with nc.Block() as block:
            block.sync(_replay(p, sems, "sync"))
            block.gpsimd(_replay(p, sems, "gpsimd"))
            block.tensor(_replay(p, sems, "tensor"))
            block.scalar(_replay(p, sems, "scalar"))
            block.vector(_replay(p, sems, "vector"))
    st_ctx.close()
    return nc


def _perm_a():
    pm = np.zeros(128, np.int64)
    pm[0:64] = np.arange(0, 128, 2)
    pm[64:128] = np.arange(1, 128, 2)
    return pm


def _perm_b():
    pm = np.zeros(128, np.int64)
    pm[0:32] = np.arange(0, 64, 2)
    pm[32:64] = np.arange(1, 64, 2)
    pm[64:128] = np.arange(64, 128)
    return pm


def _slab_pack(w):
    Dd, Ff = w.shape
    return np.ascontiguousarray(
        w.reshape(Dd // 128, 128, Ff // 128, 128).transpose(2, 1, 0, 3)
        .reshape(Ff // 128, 128, Dd))


def _vslab_pack(wv):
    out = np.zeros((2, 128, 4096), np.float32)
    for u in range(2):
        for c in range(16):
            for jj in range(2):
                j = 2 * u + jj
                out[u, :, c * 256 + jj * 128:c * 256 + jj * 128 + 128] = \
                    wv[c * 128:(c + 1) * 128, j * 128:(j + 1) * 128]
    return np.ascontiguousarray(out)


def _my_tokens(cls):
    idx = []
    for b in BLOCKS[cls]:
        idx.extend(range(b * 512, (b + 1) * 512))
    return np.array(idx, np.int64)


def _rope_tables(kind, pos):
    pos = pos.astype(np.float64)
    C = np.zeros((128, NT))
    S = np.zeros((128, NT))
    if kind == 0:
        inv = 10000.0 ** (-np.arange(0, 128, 2, dtype=np.float64) / 128.0)
        ang = inv[:, None] * pos[None, :]
        C[0:64] = np.cos(ang)
        C[64:128] = np.cos(ang)
        S[0:64] = -np.sin(ang)
        S[64:128] = np.sin(ang)
    else:
        inv = 10000.0 ** (-np.arange(0, 64, 2, dtype=np.float64) / 64.0)
        ang = inv[:, None] * pos[None, :]
        C[0:32] = np.cos(ang)
        C[32:64] = np.cos(ang)
        C[64:128] = 1.0
        S[0:32] = -np.sin(ang)
        S[32:64] = np.sin(ang)
    return C.astype(np.float32), S.astype(np.float32)


def _prm_a_mat():
    m = np.zeros((128, 128), np.float32)
    for i in range(64):
        m[64 + i, i] = 1.0
        m[i, 64 + i] = 1.0
    return m


def _prm_b_mat():
    m = np.zeros((128, 128), np.float32)
    for i in range(32):
        m[32 + i, i] = 1.0
        m[i, 32 + i] = 1.0
    return m


def _prec_fill(core):
    def key(r, pp, s):
        return (s, r // 2, BLOCKS[r % 2][pp])
    out = np.zeros((64, 8), np.float32)
    for r in range(8):
        for u in range(8):
            pp, s = u // 2, u % 2
            ku = key(r, pp, s)
            for mu in range(8):
                mp, ms = mu // 2, mu % 2
                if ku < key(core, mp, ms):
                    out[r * 8 + u, mu] = 1.0
    return out


def kernel(L0_attn_a_x, L0_attn_a_w_attn, L0_attn_a_w_proj, L0_attn_a_norm_w,
           L0_attn_b_Wq, L0_attn_b_Wk, L0_attn_b_Wv, L0_attn_b_Wo,
           L0_ffn_gate_w, L0_ffn_w1, L0_ffn_w2, L0_ffn_w3):
    x = np.asarray(L0_attn_a_x, np.float32)
    wa = np.asarray(L0_attn_a_w_attn, np.float32)
    wpa_ = np.asarray(L0_attn_a_w_proj, np.float32)
    nw = np.asarray(L0_attn_a_norm_w, np.float32)
    wq = np.asarray(L0_attn_b_Wq, np.float32)
    wk = np.asarray(L0_attn_b_Wk, np.float32)
    wv = np.asarray(L0_attn_b_Wv, np.float32)
    wo = np.asarray(L0_attn_b_Wo, np.float32)
    gwt = np.asarray(L0_ffn_gate_w, np.float32)
    w1 = np.asarray(L0_ffn_w1, np.float32)
    w2 = np.asarray(L0_ffn_w2, np.float32)
    w3 = np.asarray(L0_ffn_w3, np.float32)

    pa, pbm = _perm_a(), _perm_b()
    waq_full = wa[:, 0:2048].reshape(D, NH, 128)[:, :, pa].reshape(D, 2048)
    wak_full = wa[:, 2048:2560].reshape(D, NKV, 128)[:, :, pa].reshape(D, 512)
    wbq_full = wq.reshape(D, NH, 128)[:, :, pbm].reshape(D, 2048)
    wbk_full = wk.reshape(D, NKV, 128)[:, :, pbm].reshape(D, 512)

    shared = dict(
        waq=_slab_pack(waq_full),
        wak=_slab_pack(wak_full),
        wav=_vslab_pack(wa[:, 2560:3072]),
        wpa=_slab_pack(wpa_),
        nwa=np.ascontiguousarray(nw.reshape(DC, 128).T),
        wbq=_slab_pack(wbq_full),
        wbk=_slab_pack(wbk_full),
        wbv=_vslab_pack(wv),
        wbo=_slab_pack(wo),
        gw=np.ascontiguousarray(
            gwt.T.reshape(DC, 128, NE).transpose(1, 0, 2).reshape(
                128, DC * NE)),
        idn=np.eye(128, dtype=np.float32),
        onescol=np.ones((128, 1), np.float32),
        onesrow=np.ones((1, 128), np.float32),
        oh12=np.eye(12, dtype=np.float32),
        cbias=np.ascontiguousarray(np.tile(
            np.array([EXPB, 1e-5, 1e-6, 0.0], np.float32), (128, 1))),
        prm=np.stack([_prm_a_mat(), _prm_b_mat()]),
    )
    w13p = np.zeros((NE, 128, 8192), np.float32)
    for e in range(NE):
        for k_, wsrc in ((0, w1), (1, w3)):
            wt = wsrc[e].T
            s = wt.reshape(DC, 128, 2, 128)
            w13p[e, :, k_ * 4096:(k_ + 1) * 4096] = \
                s.transpose(1, 2, 0, 3).reshape(128, 4096)
    shared["w13"] = np.ascontiguousarray(w13p)
    w2pp = np.zeros((16, 128, 3072), np.float32)
    for e in range(NE):
        wt = w2[e].T
        for f in range(16):
            for fc in range(2):
                w2pp[f, :, (e * 2 + fc) * 128:(e * 2 + fc) * 128 + 128] = \
                    wt[fc * 128:(fc + 1) * 128, f * 128:(f + 1) * 128]
    shared["w2p"] = np.ascontiguousarray(w2pp)

    korder = np.concatenate([_my_tokens(0), _my_tokens(1)])
    kidc = np.ascontiguousarray(korder.reshape(32, 128).T.astype(np.float32))

    nc = build_nc()
    in_maps = []
    for core in range(8):
        b, cls = core // 2, core % 2
        toks = _my_tokens(cls)
        pos = toks.astype(np.float32)
        Ca, Sa = _rope_tables(0, pos)
        Cb, Sb = _rope_tables(1, pos)
        m = dict(shared)
        m["x_fm"] = np.ascontiguousarray(x[b][toks].T)
        m["csa"] = np.ascontiguousarray(np.stack([Ca, Sa]))
        m["csb"] = np.ascontiguousarray(np.stack([Cb, Sb]))
        m["qidc"] = np.ascontiguousarray(np.tile(pos[None, :], (128, 1)))
        m["kidc"] = kidc
        m["prec"] = _prec_fill(core)
        in_maps.append(m)

    res = run_bass_kernel_spmd(nc, in_maps, core_ids=list(range(8)))
    out = np.empty((4, 4096, D), np.float32)
    for core in range(8):
        b, cls = core // 2, core % 2
        out[b][_my_tokens(cls)] = res.results[core]["y_fm"].T
    return out


if __name__ == "__main__":
    print("building program...")
    nc = build_nc()
    print("instructions:", nc.n_instructions())
    print("sbuf bytes remaining:", nc.sbuf_bytes_remaining)


# version 17 (final)
# speedup vs baseline: 1.1559x; 1.0131x over previous
import numpy as np
from contextlib import ExitStack

import concourse.bass as bass
import concourse.mybir as mybir
from concourse.bass_utils import run_bass_kernel_spmd

F32 = mybir.dt.float32
BF16 = mybir.dt.bfloat16
FP16 = mybir.dt.float16
AF = mybir.ActivationFunctionType
ALU = mybir.AluOpType
AX = mybir.AxisListType

D = 2048
DC = 16
NT = 2048
NB = 4
NH = 16
NKV = 4
HD = 128
NE = 12
CAP = 1707
SCALE = 1.0 / float(np.sqrt(128.0))
EXPB = -20.0

BLOCKS = [[0, 3, 4, 7], [1, 2, 5, 6]]


def _tile_seq():
    out = []
    for c in range(2):
        for b in BLOCKS[c]:
            for q in range(4):
                out.append((b * 512 + q * 128, b * 512 + q * 128 + 128))
    return out


TILE_SEQ = _tile_seq()
TILES = []
for g in range(2):
    row = []
    for h in range(2):
        sbs = [BLOCKS[c][2 * g + h] for c in range(2)]
        lst = []
        for t, (lo, hi) in enumerate(TILE_SEQ):
            if any(lo < (sb + 1) * 512 for sb in sbs):
                need_mask = any(hi > sb * 512 + 1 for sb in sbs)
                lst.append((t, need_mask))
        row.append(lst)
    TILES.append(row)

SEMS = ("pe", "act", "dve",
        "sc", "ql0", "ql1", "kv0", "kv1", "ow0", "ow1", "qw0", "qw1",
        "kw0", "kw1", "vw", "obl", "ybl", "yw0", "yw1", "xl0", "xl1",
        "xw0", "xw1", "cw", "gl",
        "gc", "xb0", "xb1", "ws0", "ws1", "wo0", "wo1",
        "w130", "w131", "w132", "w133", "w2l", "cs", "yb2", "pla",
        "cka", "cva", "ckb", "cvb", "cf1", "cf2")


class Prog:
    ENGINES = ("sync", "gpsimd", "tensor", "scalar", "vector")

    def __init__(self):
        self.items = {e: [] for e in self.ENGINES}
        self.counts = {}
        self.waited = {}

    def sem(self, name):
        self.counts[name] = 0

    def now(self, s):
        return self.counts[s]

    AUTOSEM = {"tensor": "pe", "scalar": "act", "vector": "dve"}
    POOL_OPS = ("tensor_tensor", "tensor_copy", "tensor_scalar")

    def op(self, eng, method, *args, waits=(), inc=None, inc_by=1, **kw):
        if inc is None and method != "nop":
            inc = self.AUTOSEM.get(eng)
            if eng == "gpsimd" and method in self.POOL_OPS:
                inc = "pla"
        waits = list(waits)
        if eng in ("vector", "scalar") and method != "nop":
            es = self.AUTOSEM[eng]
            waits.append((es, self.counts[es]))
        elif eng == "gpsimd" and method in self.POOL_OPS:
            waits.append(("pla", self.counts["pla"]))
        ws = []
        seen = self.waited.setdefault(eng, {})
        for s, v in waits:
            if v <= 0 or v <= seen.get(s, 0):
                continue
            assert v <= self.counts[s], (eng, method, s, v, self.counts[s])
            seen[s] = v
            ws.append((s, v))
        if inc is not None:
            self.counts[inc] += inc_by
        self.items[eng].append((method, args, kw, ws,
                                (inc, inc_by) if inc else None))
        return self.counts[inc] if inc is not None else None

    def dma(self, eng, sem, out, in_, waits=(), **kw):
        return self.op(eng, "dma_start", out=out, in_=in_, waits=waits,
                       inc=sem, inc_by=16, **kw)

    def full(self, *names):
        return [(s, self.counts[s]) for s in names]


def _replay(prog, sems, engname):
    def run(eng):
        for method, args, kw, ws, inc in prog.items[engname]:
            for s, v in ws:
                eng.wait_ge(sems[s], v)
            inst = getattr(eng, method)(*args, **kw)
            if inc is not None:
                inst.then_inc(sems[inc[0]], inc[1])
    return run


def build_nc():
    nc = bass.Bass("TRN2", target_bir_lowering=False, debug=False)

    def din(name, shape, dt=F32):
        return nc.dram_tensor(name, shape, dt, kind="ExternalInput").ap()

    x_fm = din("x_fm", [D, NT])
    waq = din("waq", [NH, 128, D])
    wak = din("wak", [NKV, 128, D])
    wav = din("wav", [2, 128, 4096])
    wpa = din("wpa", [DC, 128, D])
    nwa = din("nwa", [128, DC])
    wbq = din("wbq", [NH, 128, D])
    wbk = din("wbk", [NKV, 128, D])
    wbv = din("wbv", [2, 128, 4096])
    wbo = din("wbo", [DC, 128, D])
    gw = din("gw", [128, DC * NE])
    w13 = din("w13", [NE, 128, 8192])
    w2p = din("w2p", [16, 128, 3072])
    CSa = din("csa", [2, 128, NT])
    CSb = din("csb", [2, 128, NT])
    prm = din("prm", [2, 128, 128])
    qidc = din("qidc", [128, NT])
    kidc = din("kidc", [128, 32])
    idn = din("idn", [128, 128])
    onescol = din("onescol", [128, 1])
    onesrow = din("onesrow", [1, 128])
    oh12 = din("oh12", [12, 12])
    prec = din("prec", [64, 8])
    cbias = din("cbias", [128, 4])

    y_fm = nc.dram_tensor("y_fm", [D, NT], F32, kind="ExternalOutput").ap()

    Xd = nc.dram_tensor("Xd", [D, NT], F32).ap()
    Qd = nc.dram_tensor("Qd", [D, NT], BF16).ap()
    Od = nc.dram_tensor("Od", [D, NT], BF16).ap()
    Yd = nc.dram_tensor("Yd", [D, NT], BF16).ap()
    kown = nc.dram_tensor("kown", [512, NT], BF16)
    vown = nc.dram_tensor("vown", [512, NT], BF16)
    kg = nc.dram_tensor("kg", [1024, NT], BF16)
    vg = nc.dram_tensor("vg", [1024, NT], BF16)
    contrib = nc.dram_tensor("contrib", [1, 96], F32)
    gath = nc.dram_tensor("gath", [8, 96], F32)

    p = Prog()
    for s in SEMS:
        p.sem(s)

    st_ctx = ExitStack()
    sb = lambda n, sh, dt: st_ctx.enter_context(nc.sbuf_tensor(n, sh, dt))

    xln = sb("xln", [128, DC * NT], BF16)
    PB = sb("PB", [128, 24576], BF16)
    PF = sb("PF", [128, 3072], F32)
    GS = sb("GS", [128, 24576], BF16)
    ROWS = sb("ROWS", [1, 4096], F32)
    RT1 = sb("RT1", [12, NT], FP16)
    RT2 = sb("RT2", [12, NT], FP16)
    OS1 = sb("OS1", [128, 192], BF16)
    OS2 = sb("OS2", [128, 192], BF16)
    id_bf = sb("id_bf", [128, 128], BF16)
    RWS = sb("RWS", [128, 32], F32)
    rs2_sb = sb("rs2_sb", [12, 512], F32)
    id_sb = sb("id_sb", [128, 128], F32)
    prm_sb = sb("prm_sb", [128, 256], BF16)
    onesc = sb("onesc", [128, 1], BF16)
    onesc32 = sb("onesc32", [128, 1], F32)
    onesr = sb("onesr", [1, 128], F32)
    oh_sb = sb("oh_sb", [12, 12], F32)
    prec_sb = sb("prec_sb", [64, 8], F32)
    gw_sb = sb("gw_sb", [128, DC * NE], BF16)
    nwa_sb = sb("nwa_sb", [128, DC], F32)
    cb_sb = sb("cb_sb", [128, 4], F32)
    kid_sb = sb("kid_sb", [128, 32], F32)
    g64_sb = sb("g64_sb", [64, 12], F32)
    cnts_sb = sb("cnts_sb", [12, 8], F32)
    thr_sb = sb("thr_sb", [12, 8], F32)
    rs_sb = sb("rs_sb", [12, 512], F32)
    sel = sb("sel", [128, 96], F32)
    rw_sb = sb("rw_sb", [128, 8], F32)
    psA = st_ctx.enter_context(nc.psum_tensor("psA", [128, 4096], F32))

    def pb(o, n):
        return PB[:, o:o + n]

    def pf(o, n):
        return PF[:, o:o + n]

    qidc_sb = pf(0, 2048)

    for t, src in ((qidc_sb, qidc), (id_sb[:], idn), (onesr[:], onesrow),
                   (oh_sb[:], oh12), (prec_sb[:], prec), (nwa_sb[:], nwa),
                   (cb_sb[:], cbias), (kid_sb[:], kidc),
                   (onesc32[:], onescol)):
        p.dma("sync", "sc", t, src)
    for t, src in ((prm_sb[:, 0:128], prm[0]), (prm_sb[:, 128:256], prm[1]),
                   (onesc[:], onescol), (gw_sb[:], gw), (id_bf[:], idn)):
        p.dma("gpsimd", "gc", t, src)
    CONST_S = p.full("sc")
    CONST_G = p.full("gc")

    state = {("xw", 0): 0, ("xw", 1): 0}

    def emit_ln(xsrc, gate):
        xb = [pb(0, 2048), pb(2048, 2048)]
        xq = [pb(4096, 2048), pb(6144, 2048)]
        xb_pe = [0, 0]
        xb_act = [0, 0]
        xq_pe = [0, 0]
        for c in range(DC):
            i = c % 2
            w = [("pe", xb_pe[i]), ("act", xb_act[i])]
            if c == 0:
                w += gate
            ld = p.dma("gpsimd", f"xb{i}", xb[i],
                       xsrc[c * 128:(c + 1) * 128, :], waits=w)
            xb_act[i] = p.op("scalar", "activation", xq[i], xb[i], AF.Square,
                             waits=[(f"xb{i}", ld), ("pe", xq_pe[i])],
                             inc="act")
            for jj in range(4):
                xb_pe[i] = p.op(
                    "tensor", "matmul", psA[0:1, jj * 512:jj * 512 + 512],
                    onesc[:], xb[i][:, jj * 512:jj * 512 + 512],
                    start=(c == 0), stop=(c == DC - 1), skip_group_check=True,
                    waits=[(f"xb{i}", ld)] + CONST_G if jj == 0 else [],
                    inc="pe" if jj == 3 else None)
            for jj in range(4):
                xq_pe[i] = p.op(
                    "tensor", "matmul",
                    psA[0:1, 2048 + jj * 512:2048 + jj * 512 + 512],
                    onesc[:], xq[i][:, jj * 512:jj * 512 + 512],
                    start=(c == 0), stop=(c == DC - 1), skip_group_check=True,
                    waits=[("act", xb_act[i])] if jj == 0 else [],
                    inc="pe" if jj == 3 else None)
        stats_pe = p.now("pe")
        mrow = ROWS[0:1, 0:2048]
        vrow = ROWS[0:1, 2048:4096]
        p.op("vector", "tensor_scalar", mrow, psA[0:1, 0:2048], 1.0 / 2048.0,
             None, ALU.mult, waits=[("pe", stats_pe)])
        p.op("vector", "tensor_tensor", vrow, mrow, mrow, ALU.mult)
        rows_dve = p.op("vector", "scalar_tensor_tensor", out=vrow,
                        in0=psA[0:1, 2048:4096], scalar=1.0 / 2048.0,
                        in1=vrow, op0=ALU.mult, op1=ALU.subtract, inc="dve")
        p.op("scalar", "activation", vrow, vrow, AF.Ln, bias=cb_sb[0:1, 1:2],
             waits=[("dve", rows_dve)] + CONST_S)
        rstd_act = p.op("scalar", "activation", vrow, vrow, AF.Exp,
                        scale=-0.5, inc="act")
        for jj in range(4):
            p.op("tensor", "matmul", psA[:, jj * 512:jj * 512 + 512],
                 onesr[:], mrow[0:1, jj * 512:jj * 512 + 512], start=True,
                 stop=True, skip_group_check=True,
                 waits=[("dve", rows_dve)] + CONST_S if jj == 0 else [])
            bc_pe = p.op("tensor", "matmul",
                         psA[:, 2048 + jj * 512:2048 + jj * 512 + 512],
                         onesr[:], vrow[0:1, jj * 512:jj * 512 + 512],
                         start=True, stop=True, skip_group_check=True,
                         waits=[("act", rstd_act)] if jj == 0 else [],
                         inc="pe" if jj == 3 else None)
        m_b = pb(8192, 2048)
        r_b = pb(10240, 2048)
        p.op("vector", "tensor_copy", m_b, psA[:, 0:2048],
             waits=[("pe", bc_pe)])
        bc_dve = p.op("vector", "tensor_copy", r_b, psA[:, 2048:4096],
                      inc="dve")
        tmp = pb(4096, 2048)
        ap_dve = [0, 0]
        last = bc_dve
        for c in range(DC):
            i = c % 2
            ld = p.dma("gpsimd", f"xb{i}", xb[i],
                       xsrc[c * 128:(c + 1) * 128, :],
                       waits=[("pe", xb_pe[i]), ("act", xb_act[i]),
                              ("dve", ap_dve[i])])
            p.op("vector", "tensor_tensor", tmp, xb[i], m_b, ALU.subtract,
                 waits=[(f"xb{i}", ld), ("dve", bc_dve)])
            last = p.op("vector", "tensor_tensor",
                        xln[:, c * 2048:(c + 1) * 2048], tmp, r_b, ALU.mult,
                        inc="dve")
            ap_dve[i] = last
            xb_pe[i] = 0
            xb_act[i] = 0
        return last

    wsl = [pb(12288, 2048), pb(14336, 2048)]
    wsl_rd = [0, 0]

    def proj_head(fidx, wsrc, xln_dve, epilog):
        i = fidx % 2
        lw = p.dma("gpsimd", f"ws{i}", wsl[i], wsrc,
                   waits=[("pe", wsl_rd[i])])
        for blk in range(NB):
            sl = blk % 2
            pp = psA[:, sl * 512:sl * 512 + 512]
            for c in range(DC):
                w = []
                if c == 0:
                    w = [(f"ws{i}", lw), ("dve", xln_dve),
                         ("act", state.get(("ppfree", sl), 0))]
                pe_c = p.op(
                    "tensor", "matmul", pp,
                    wsl[i][:, c * 128:c * 128 + 128],
                    xln[:, c * 2048 + blk * 512:c * 2048 + blk * 512 + 512],
                    start=(c == 0), stop=(c == DC - 1), skip_group_check=True,
                    waits=w, inc="pe" if c == DC - 1 else None)
            if blk >= NB - 2:
                wsl_rd[i] = pe_c
            epilog(blk, pp, pe_c)

    RP = dict(tq_w=[[], []], qo_w=[[], []], pr_w=[], cs_n=0)

    def rope_epilog(layer, dst_of_blk, wstream):
        CS = pb(16384, 4096)
        tq = [pb(20480, 512), pb(20992, 512)]
        qo = [pb(21504, 512), pb(22016, 512)]
        t1 = pb(22528, 512)
        t2 = pb(23040, 512)
        prmv = prm_sb[:, layer * 128:layer * 128 + 128]

        def ep(blk, pp, pe_c):
            i = blk % 2
            ev = p.op("scalar", "copy", tq[i], pp,
                      waits=[("pe", pe_c)] + RP["tq_w"][i], inc="act")
            state[("ppfree", blk % 2)] = ev
            sw = p.op("tensor", "matmul", psA[:, 1024:1536], prmv, tq[i],
                      start=True, stop=True, skip_group_check=True,
                      waits=[("act", ev)] + CONST_G + RP["pr_w"], inc="pe")
            t1c = p.op("vector", "tensor_tensor", t1, tq[i],
                       CS[:, blk * 512:blk * 512 + 512], ALU.mult,
                       waits=[("act", ev), ("cs", RP["cs_n"])], inc="dve")
            t2c = p.op("vector", "tensor_tensor", t2, psA[:, 1024:1536],
                       CS[:, 2048 + blk * 512:2048 + blk * 512 + 512],
                       ALU.mult, waits=[("pe", sw)], inc="dve")
            RP["pr_w"] = [("dve", t2c)]
            qoc = p.op("vector", "tensor_tensor", qo[i], t1, t2, ALU.add,
                       waits=RP["qo_w"][i], inc="dve")
            RP["tq_w"][i] = [("pe", sw), ("dve", t1c)]
            wn = p.dma("sync", f"{wstream}{i}", dst_of_blk(blk), qo[i],
                       waits=[("dve", qoc)])
            RP["qo_w"][i] = [(f"{wstream}{i}", wn)]
        return ep

    def emit_vproj(wv_src, xln_dve):
        vo_sb = pb(4096, 8192)
        pv_dve = [0, 0]
        for u in range(2):
            a, b_ = u % 2, 1 - (u % 2)
            la = p.dma("gpsimd", f"ws{a}", wsl[a], wv_src[u][:, 0:2048],
                       waits=[("pe", wsl_rd[a])])
            lb = p.dma("gpsimd", f"ws{b_}", wsl[b_], wv_src[u][:, 2048:4096],
                       waits=[("pe", wsl_rd[b_])])
            for tt in range(16):
                sl = tt % 2
                pv = psA[:, 1536 + sl * 512:1536 + sl * 512 + 256]
                for c in range(DC):
                    w = [(f"ws{a}", la), (f"ws{b_}", lb), ("dve", xln_dve),
                         ("dve", pv_dve[sl])] if c == 0 else []
                    slab = wsl[a] if c < 8 else wsl[b_]
                    cc = c % 8
                    pe_c = p.op(
                        "tensor", "matmul", pv,
                        xln[:, c * 2048 + tt * 128:
                            c * 2048 + tt * 128 + 128],
                        slab[:, cc * 256:cc * 256 + 256],
                        start=(c == 0), stop=(c == DC - 1),
                        skip_group_check=True, waits=w,
                        inc="pe" if c == DC - 1 else None)
                if tt >= 14:
                    wsl_rd[a] = pe_c
                    wsl_rd[b_] = pe_c
                d = 0
                for jj in range(2):
                    j = 2 * u + jj
                    d = p.op("vector", "tensor_copy",
                             vo_sb[:, j * 2048 + tt * 128:
                                   j * 2048 + tt * 128 + 128],
                             pv[:, jj * 128:jj * 128 + 128],
                             waits=[("pe", pe_c)] if jj == 0 else [],
                             inc="dve" if jj == 1 else None)
                pv_dve[sl] = d
        dv = p.now("dve")
        for j in range(4):
            p.dma("sync", "vw", vown.ap()[j * 128:(j + 1) * 128, :],
                  vo_sb[:, j * 2048:(j + 1) * 2048],
                  waits=[("dve", dv)] if j == 0 else [])
        return p.full("vw")

    def emit_core(layer):
        ck, cv = ("cka", "cva") if layer == 0 else ("ckb", "cvb")
        Kb = [pb(0, 4096), pb(8192, 4096)]
        Vb = [pb(4096, 4096), pb(12288, 4096)]
        qb = [pb(16384, 1024), pb(17408, 1024)]
        eb = [pb(18432, 1024), pb(19456, 1024)]
        ob = [pb(20480, 1024), pb(21504, 1024)]
        ou = [pb(22528, 1024), pb(23552, 1024)]
        ou_rd = [0, 0]
        rb = pf(2048, 1024)
        av = psA[:, 2048:3072]
        sm = psA[0:1, 3072:4096]
        if layer == 0:
            NS = 4
            sts = [psA[:, k * 512:k * 512 + 512] for k in range(4)]
            ebs = [eb[0][:, 0:512], eb[0][:, 512:1024],
                   eb[1][:, 0:512], eb[1][:, 512:1024]]
            accs = [GS[:, 0:512], GS[:, 512:1024]]
        else:
            NS = 2
            sts = [psA[:, 0:1024], psA[:, 1024:2048]]
            ebs = [eb[0][:, 0:1024], eb[1][:, 0:1024]]
            acc = GS[:, 0:1024]
        kv_n = [0, 0]
        kv_rd = [0, 0]
        q_n = [0, 0]
        q_rd = [0, 0]
        ow_n = [0, 0]
        exp_done = [0] * NS
        av_rd = [0] * NS
        acc_n = [0] * NS
        sm_rd = [0]
        ki = qi = 0
        rbev = 0
        av_free = 0
        sm_free = 0
        qd_done = state["qd_done"]

        def load_kv(j, slot, first=False):
            w = [("pe", kv_rd[slot]), (ck, p.now(ck)), (cv, p.now(cv))]
            if first:
                w += qd_done
            p.dma("sync", f"kv{slot}", Kb[slot][:, 0:2048],
                  kg.ap()[j * 128:(j + 1) * 128, :], waits=w)
            p.dma("sync", f"kv{slot}", Kb[slot][:, 2048:4096],
                  kg.ap()[512 + j * 128:512 + j * 128 + 128, :])
            p.dma("sync", f"kv{slot}", Vb[slot][:, 0:2048],
                  vg.ap()[j * 128:(j + 1) * 128, :])
            p.dma("sync", f"kv{slot}", Vb[slot][:, 2048:4096],
                  vg.ap()[512 + j * 128:512 + j * 128 + 128, :])
            return p.now(f"kv{slot}")

        kv_n[0] = load_kv(0, 0, first=True)
        for g in range(2):
            for j in range(NKV):
                slot = (g * NKV + j) % 2
                for qh in range(4):
                    h = j * 4 + qh
                    qsl = qi % 2
                    q_n[qsl] = p.dma(
                        "sync", f"ql{qsl}", qb[qsl],
                        Qd[h * 128:(h + 1) * 128, g * 1024:g * 1024 + 1024],
                        waits=[("pe", q_rd[qsl])] + qd_done)
                    if qh == 0 and (g, j) != (1, NKV - 1):
                        nj, ng = (j + 1, g) if j < NKV - 1 else (0, g + 1)
                        kv_n[1 - slot] = load_kv(nj, 1 - slot)
                    if layer == 0:
                        tiles = []
                        for h01 in range(2):
                            n = len(TILES[g][h01])
                            for idx, (t, msk) in enumerate(TILES[g][h01]):
                                tiles.append((h01, t, msk, idx == n - 1))
                    else:
                        tiles = [(None, t, False, t == 31)
                                 for t in range(32)]
                    pend = None
                    first = {0: True, 1: True, None: True}

                    def flush(pd):
                        nonlocal av_free, sm_free
                        sp, h01, t, gate, fa, la = pd
                        w = [gate]
                        if fa:
                            w += [("act", av_free), ("act", sm_free)]
                        if layer == 0:
                            off = h01 * 512
                            av_rd[sp] = p.op(
                                "tensor", "matmul", av[:, off:off + 512],
                                Vb[slot][:, t * 128:t * 128 + 128],
                                ebs[sp], start=fa, stop=la,
                                skip_group_check=True, waits=w, inc="pe")
                            if fa:
                                acc_n[sp] = p.op(
                                    "vector", "tensor_copy", accs[h01],
                                    ebs[sp],
                                    waits=[gate, ("pe", sm_rd[0])],
                                    inc="dve")
                            else:
                                acc_n[sp] = p.op(
                                    "vector", "tensor_tensor", accs[h01],
                                    accs[h01], ebs[sp], ALU.add,
                                    waits=[gate], inc="dve")
                        else:
                            for jj in range(2):
                                av_rd[sp] = p.op(
                                    "tensor", "matmul",
                                    av[:, jj * 512:jj * 512 + 512],
                                    Vb[slot][:, t * 128:t * 128 + 128],
                                    ebs[sp][:, jj * 512:jj * 512 + 512],
                                    start=fa, stop=la, skip_group_check=True,
                                    waits=w if jj == 0 else [],
                                    inc="pe" if jj == 1 else None)
                            if fa:
                                acc_n[sp] = p.op(
                                    "vector", "tensor_copy", acc,
                                    ebs[sp],
                                    waits=[gate, ("pe", sm_rd[0])],
                                    inc="dve")
                            else:
                                acc_n[sp] = p.op(
                                    "vector", "tensor_tensor", acc, acc,
                                    ebs[sp], ALU.add,
                                    waits=[gate], inc="dve")

                    for (h01, t, msk, is_last) in tiles:
                        sp = ki % NS
                        ww = [("act", exp_done[sp]), (f"ql{qsl}", q_n[qsl]),
                              (f"kv{slot}", kv_n[slot]), ("act", rbev)]
                        if layer == 0:
                            qk = p.op("tensor", "matmul", sts[sp],
                                      Kb[slot][:, t * 128:t * 128 + 128],
                                      qb[qsl][:, h01 * 512:h01 * 512 + 512],
                                      start=True, stop=True,
                                      skip_group_check=True, waits=ww,
                                      inc="pe")
                        else:
                            p.op("tensor", "matmul", sts[sp][:, 0:512],
                                 Kb[slot][:, t * 128:t * 128 + 128],
                                 qb[qsl][:, 0:512], start=True, stop=True,
                                 skip_group_check=True, waits=ww)
                            qk = p.op("tensor", "matmul",
                                      sts[sp][:, 512:1024],
                                      Kb[slot][:, t * 128:t * 128 + 128],
                                      qb[qsl][:, 512:1024], start=True,
                                      stop=True, skip_group_check=True,
                                      inc="pe")
                        if pend is not None:
                            flush(pend)
                        ex = p.op("scalar", "activation", ebs[sp],
                                  sts[sp], AF.Exp,
                                  bias=cb_sb[:, 0:1], scale=SCALE,
                                  waits=[("pe", qk), ("pe", av_rd[sp]),
                                         ("dve", acc_n[sp])]
                                  + CONST_S, inc="act")
                        exp_done[sp] = ex
                        gate = ("act", ex)
                        if layer == 0 and msk:
                            blkcol = (2 * g + h01) * 512
                            mk = p.op("vector", "scalar_tensor_tensor",
                                      out=ebs[sp],
                                      in0=qidc_sb[:, blkcol:blkcol + 512],
                                      scalar=kid_sb[:, t:t + 1],
                                      in1=ebs[sp],
                                      op0=ALU.is_ge, op1=ALU.mult,
                                      waits=[("act", ex)], inc="dve")
                            gate = ("dve", mk)
                        pend = (sp, h01, t, gate, first[h01], is_last)
                        first[h01] = False
                        ki += 1
                    flush(pend)
                    accq = p.now("dve")
                    for jj in range(2):
                        smw = [("dve", accq), ("act", sm_free)] \
                            if jj == 0 else []
                        if layer == 0:
                            smsrc = accs[jj]
                        else:
                            smsrc = acc[:, jj * 512:jj * 512 + 512]
                        smlhs = onesc[:]
                        p.op("tensor", "matmul",
                             sm[0:1, jj * 512:jj * 512 + 512], smlhs, smsrc,
                             start=True, stop=True, skip_group_check=True,
                             waits=smw, inc="pe" if jj == 1 else None)
                    last_pe = p.now("pe")
                    sm_rd[0] = last_pe
                    q_rd[qsl] = last_pe
                    kv_rd[slot] = last_pe
                    p.op("scalar", "activation", ROWS[0:1, 0:1024], sm,
                         AF.Ln, waits=[("pe", last_pe)])
                    recip = p.op("scalar", "activation", ROWS[0:1, 0:1024],
                                 ROWS[0:1, 0:1024], AF.Exp, scale=-1.0,
                                 inc="act")
                    sm_free = recip
                    for jj in range(2):
                        bc_pe = p.op(
                            "tensor", "matmul",
                            psA[:, jj * 512:jj * 512 + 512], onesr[:],
                            ROWS[0:1, jj * 512:jj * 512 + 512], start=True,
                            stop=True, skip_group_check=True,
                            waits=([("act", recip)]
                                   + [("act", e) for e in exp_done])
                            if jj == 0 else [],
                            inc="pe" if jj == 1 else None)
                    rbev = p.op("scalar", "copy", rb, psA[:, 0:1024],
                                waits=[("pe", bc_pe)], inc="act")
                    exp_done[0] = rbev
                    if NS == 4:
                        exp_done[1] = rbev
                    osl = qi % 2
                    oun = p.op("scalar", "copy", ou[osl], av,
                               waits=[("pe", last_pe),
                                      ("dve", ou_rd[osl])], inc="act")
                    av_free = oun
                    osc = p.op("vector", "tensor_tensor", ob[osl], ou[osl],
                               rb, ALU.mult,
                               waits=[("act", rbev), ("act", oun),
                                      (f"ow{osl}", ow_n[osl])], inc="dve")
                    ou_rd[osl] = osc
                    ow_n[osl] = p.dma(
                        "sync", f"ow{osl}",
                        Od[h * 128:(h + 1) * 128, g * 1024:g * 1024 + 1024],
                        ob[osl], waits=[("dve", osc)])
                    qi += 1
                if g == 0:
                    state["od_g0"] = p.full("ow0", "ow1")
        state["od_done"] = p.full("ow0", "ow1")

    def emit_wo(layer, wsrc, xsrc, xdst, gate):
        oblk = pb(0, 16384)
        wslo = [pb(16384, 2048), pb(18432, 2048)]
        wslo_rd = [0, 0]
        ysl = [pb(20480, 512), pb(20992, 512)]
        yw_n = [0, 0]
        xo = [pf(2048, 512), pf(2560, 512)]
        wofree = [0, 0]
        od_done = state["od_done"]
        od_g0 = state["od_g0"]
        oblk_rd = 0
        for tp in range(2):
            for c in range(DC):
                w = ((od_g0 if tp == 0 else od_done)
                     + [("pe", oblk_rd)]) if c == 0 else []
                p.dma("sync", "obl", oblk[:, c * 1024:(c + 1) * 1024],
                      Od[c * 128:(c + 1) * 128, tp * 1024:tp * 1024 + 1024],
                      waits=w)
            oq = p.now("obl")
            for f in range(DC):
                i = f % 2
                w = [("pe", wslo_rd[i])]
                if tp == 0 and f == 0:
                    w += od_done
                lw = p.dma("gpsimd", f"wo{i}", wslo[i], wsrc[f], waits=w)

                for tb in range(2):
                    blk = tp * 2 + tb
                    sl = blk % 2
                    pp = psA[:, sl * 512:sl * 512 + 512]
                    for c in range(DC):
                        w = [(f"wo{i}", lw), ("obl", oq),
                             (("act", wofree[sl]) if layer == 0
                              else ("dve", wofree[sl]))] if c == 0 else []
                        pe_c = p.op(
                            "tensor", "matmul", pp,
                            wslo[i][:, c * 128:c * 128 + 128],
                            oblk[:, c * 1024 + tb * 512:
                                 c * 1024 + tb * 512 + 512],
                            start=(c == 0), stop=(c == DC - 1),
                            skip_group_check=True, waits=w,
                            inc="pe" if c == DC - 1 else None)
                    if tb == 1:
                        wslo_rd[i] = pe_c
                    if layer == 0:
                        ev = p.op("scalar", "copy", ysl[sl], pp,
                                  waits=[("pe", pe_c),
                                         (f"yw{sl}", yw_n[sl])], inc="act")
                        wofree[sl] = ev
                        yw_n[sl] = p.dma(
                            "sync", f"yw{sl}",
                            Yd[f * 128:(f + 1) * 128,
                               blk * 512:blk * 512 + 512], ysl[sl],
                            waits=[("act", ev)])
                    else:
                        w = [(f"xw{sl}", state[("xw", sl)])]
                        if tp == 0 and f == 0 and tb == 0:
                            w += gate
                        xq = p.dma("sync", f"xl{sl}", xo[sl],
                                   xsrc[f * 128:(f + 1) * 128,
                                        blk * 512:blk * 512 + 512], waits=w)
                        ad = p.op("vector", "tensor_tensor", xo[sl], xo[sl],
                                  pp, ALU.add,
                                  waits=[("pe", pe_c), (f"xl{sl}", xq)],
                                  inc="dve")
                        wofree[sl] = ad
                        state[("xw", sl)] = p.dma(
                            "sync", f"xw{sl}",
                            xdst[f * 128:(f + 1) * 128,
                                 blk * 512:blk * 512 + 512], xo[sl],
                            waits=[("dve", ad)])
            oblk_rd = p.now("pe")
            if tp == 0:
                state["yd_tp0"] = p.full("yw0", "yw1")
        return p.full("xw0", "xw1", "yw0", "yw1")

    def emit_rms(xsrc, xdst, gate):
        yblk = [pb(0, 8192), pb(10240, 8192)]
        ysq = [pb(8192, 512), pb(8704, 512)]
        rrb = pb(9216, 512)
        t3 = pb(9728, 512)
        xo = [pf(2048, 512), pf(2560, 512)]
        ysq_pe = [0, 0]
        ybq = [0, 0]
        yb_act = [0, 0]
        yb_dve = [0, 0]
        yd_tp0 = state["yd_tp0"]
        yd_done = state["yd_done"]
        rr_dve = 0
        streams = ["ybl", "yb2"]

        def load_yblk(tb, i):
            w = [("act", yb_act[i]), ("dve", yb_dve[i])]
            w += yd_tp0 if tb < 2 else yd_done
            for c in range(DC):
                p.dma("sync", streams[i], yblk[i][:, c * 512:(c + 1) * 512],
                      Yd[c * 128:(c + 1) * 128, tb * 512:tb * 512 + 512],
                      waits=w if c == 0 else [])
            return p.now(streams[i])

        ybq[0] = load_yblk(0, 0)
        for tb in range(NB):
            i = tb % 2
            if tb < NB - 1:
                ybq[1 - i] = load_yblk(tb + 1, 1 - i)
            for c in range(DC):
                ii = c % 2
                sq = p.op("scalar", "activation", ysq[ii],
                          yblk[i][:, c * 512:(c + 1) * 512], AF.Square,
                          waits=[(streams[i], ybq[i]), ("pe", ysq_pe[ii])],
                          inc="act")
                ysq_pe[ii] = p.op(
                    "tensor", "matmul", psA[0:1, 1024:1536], onesc[:],
                    ysq[ii], start=(c == 0), stop=(c == DC - 1),
                    skip_group_check=True,
                    waits=[("act", sq)] + ([("dve", rr_dve)] if c == 0
                                           else []), inc="pe")
            spe = p.now("pe")
            yb_act[i] = p.now("act")
            mr = ROWS[0:1, 0:512]
            lr = ROWS[0:1, 512:1024]
            d0 = p.op("vector", "tensor_scalar", mr, psA[0:1, 1024:1536],
                      1.0 / 2048.0, None, ALU.mult, waits=[("pe", spe)],
                      inc="dve")
            p.op("scalar", "activation", lr, mr, AF.Ln, bias=cb_sb[0:1, 2:3],
                 waits=[("dve", d0)])
            ra = p.op("scalar", "activation", mr, lr, AF.Exp, scale=-0.5,
                      inc="act")
            rpe = p.op("tensor", "matmul", psA[:, 1536:2048], onesr[:], mr,
                       start=True, stop=True, skip_group_check=True,
                       waits=[("act", ra), ("dve", d0)], inc="pe")
            rr_dve = p.op("vector", "tensor_copy", rrb, psA[:, 1536:2048],
                          waits=[("pe", rpe)], inc="dve")
            for c in range(DC):
                sl = c % 2
                w = [(f"xw{sl}", state[("xw", sl)])]
                if tb == 0 and c == 0:
                    w += gate
                xq = p.dma("sync", f"xl{sl}", xo[sl],
                           xsrc[c * 128:(c + 1) * 128,
                                tb * 512:tb * 512 + 512], waits=w)
                p.op("vector", "scalar_tensor_tensor", out=t3,
                     in0=yblk[i][:, c * 512:(c + 1) * 512],
                     scalar=nwa_sb[:, c:c + 1], in1=rrb,
                     op0=ALU.mult, op1=ALU.mult,
                     waits=[("dve", rr_dve)] + CONST_S)
                ad = p.op("vector", "tensor_tensor", xo[sl], xo[sl], t3,
                          ALU.add, waits=[(f"xl{sl}", xq)], inc="dve")
                state[("xw", sl)] = p.dma(
                    "sync", f"xw{sl}",
                    xdst[c * 128:(c + 1) * 128, tb * 512:tb * 512 + 512],
                    xo[sl], waits=[("dve", ad)])
            yb_dve[i] = p.now("dve")
        return p.full("xw0", "xw1")

    def emit_ffn(xln_dve, ffn_idx, xsrc, xdst, gate):
        ccs = "cf1" if ffn_idx == 0 else "cf2"
        lg = [psA[:, 0:12], psA[:, 512:524]]
        tp = [psA[0:12, 1024:1152], psA[0:12, 1536:1664]]
        Es = sel[:, 0:12]
        O1 = sel[:, 12:24]
        O2 = sel[:, 24:36]
        csA = sel[:, 36:48]
        csB = sel[:, 48:60]
        E2 = sel[:, 60:72]
        inv = sel[:, 72:84]
        m1 = rw_sb[:, 0:1]
        m1n = rw_sb[:, 1:2]
        m2e = rw_sb[:, 2:3]
        den = rw_sb[:, 3:4]
        rw1 = rw_sb[:, 4:5]
        rw2 = rw_sb[:, 5:6]
        lg_free = [(0, 0), (0, 0)]
        tp_free = [0, 0]
        for tt in range(16):
            i = tt % 2
            for c in range(DC):
                w = [("dve", xln_dve), ("dve", lg_free[i][0]),
                     ("act", lg_free[i][1])] + CONST_G if c == 0 else []
                lpe = p.op(
                    "tensor", "matmul", lg[i],
                    xln[:, c * 2048 + tt * 128:c * 2048 + tt * 128 + 128],
                    gw_sb[:, c * 12:c * 12 + 12], start=(c == 0),
                    stop=(c == DC - 1), skip_group_check=True, waits=w,
                    inc="pe" if c == DC - 1 else None)
            p.op("vector", "tensor_reduce", m1, lg[i], AX.X, ALU.max,
                 waits=[("pe", lpe)])
            p.op("vector", "tensor_scalar", m1n, m1, -1.0, None, ALU.mult)
            d = p.op("vector", "tensor_scalar", O1, lg[i], m1, None,
                     ALU.is_ge, inc="dve")
            sa = p.op("scalar", "activation", Es, lg[i], AF.Exp, bias=m1n,
                      waits=[("dve", d)], inc="act")
            p.op("vector", "tensor_tensor_scan", out=csA, data0=O1,
                 data1=O1, initial=0.0, op0=ALU.add, op1=ALU.max)
            p.op("vector", "tensor_tensor", csB, csA, O1, ALU.subtract)
            p.op("vector", "tensor_scalar", csB, csB, 0.5, None, ALU.is_lt)
            p.op("vector", "tensor_tensor", O1, O1, csB, ALU.mult)
            p.op("vector", "tensor_scalar", inv, O1, -1.0, 1.0, ALU.mult,
                 ALU.add)
            p.op("vector", "tensor_tensor", E2, Es, inv, ALU.mult,
                 waits=[("act", sa)])
            p.op("vector", "tensor_reduce", m2e, E2, AX.X, ALU.max)
            p.op("vector", "tensor_scalar", O2, E2, m2e, None, ALU.is_ge)
            p.op("vector", "tensor_tensor_scan", out=csA, data0=O2,
                 data1=O2, initial=0.0, op0=ALU.add, op1=ALU.max)
            p.op("vector", "tensor_tensor", csB, csA, O2, ALU.subtract)
            p.op("vector", "tensor_scalar", csB, csB, 0.5, None, ALU.is_lt)
            p.op("vector", "tensor_tensor", O2, O2, csB, ALU.mult)
            p.op("vector", "tensor_scalar", den, m2e, 1.0, None, ALU.add)
            p.op("vector", "reciprocal", rw1, den)
            p.op("vector", "tensor_scalar", rw2, rw1, -1.0, 1.0, ALU.mult,
                 ALU.add)
            p.op("vector", "tensor_copy", OS1[:, tt * 12:tt * 12 + 12], O1)
            p.op("vector", "tensor_copy", OS2[:, tt * 12:tt * 12 + 12], O2)
            p.op("vector", "tensor_copy", RWS[:, 2 * tt:2 * tt + 1], rw1)
            d1 = p.op("vector", "tensor_copy", RWS[:, 2 * tt + 1:2 * tt + 2],
                      rw2, inc="dve")
            lg_free[i] = (d1, sa)
            tr1 = p.op("tensor", "transpose", tp[i][0:12, 0:128], O1,
                       id_sb[:], waits=[("dve", d1), ("dve", tp_free[i])]
                       + CONST_S, inc="pe")
            e1 = p.op("vector", "tensor_copy",
                      RT1[:, tt * 128:tt * 128 + 128], tp[i][0:12, :],
                      waits=[("pe", tr1)], inc="dve")
            tr2 = p.op("tensor", "transpose", tp[i][0:12, 0:128], O2,
                       id_sb[:], waits=[("dve", e1)], inc="pe")
            tp_free[i] = p.op("vector", "tensor_copy",
                              RT2[:, tt * 128:tt * 128 + 128],
                              tp[i][0:12, :], waits=[("pe", tr2)], inc="dve")
        sel_done = p.now("dve")
        for bpos in range(4):
            for s in range(2):
                RT = RT1 if s == 0 else RT2
                u = bpos * 2 + s
                cd = p.op("vector", "tensor_reduce", cnts_sb[:, u:u + 1],
                          RT[:, bpos * 512:bpos * 512 + 512], AX.X, ALU.add,
                          inc="dve" if (bpos == 3 and s == 1) else None)
        cq = p.dma("sync", "cw",
                   contrib.ap().rearrange("o (u e) -> (o e) u", u=8, e=12),
                   cnts_sb[:], waits=[("dve", cd)],
                   allow_slow_non_contiguous=True)
        p.op("gpsimd", "collective_compute", "AllGather", ALU.bypass,
             replica_groups=[list(range(8))],
             ins=[contrib.ap().opt()], outs=[gath.ap().opt()],
             waits=[("cw", cq)], inc=ccs)
        M = PF[0:12, 0:2048]
        m_dve = [0]

        def emit_mphase():
            gq = p.dma("sync", "gl", g64_sb[:],
                       gath.ap().rearrange("r (u e) -> (r u) e", u=8, e=12),
                       waits=[(ccs, p.now(ccs))])
            bmm = p.op("tensor", "matmul", psA[0:12, 2048:2056], g64_sb[:],
                       prec_sb[:], start=True, stop=True,
                       skip_group_check=True,
                       waits=[("gl", gq)] + CONST_S, inc="pe")
            thr_dve = p.op("vector", "tensor_scalar", thr_sb[:],
                           psA[0:12, 2048:2056], -1.0, float(CAP) + 0.5,
                           ALU.mult, ALU.add, waits=[("pe", bmm)], inc="dve")
            kp1 = psA[:, 1024:1036]
            kp2 = psA[:, 1536:1548]
            mp = psA[0:12, 2048:2176]
            ktm1 = sel[:, 36:48]
            ktm2 = sel[:, 48:60]
            tA = sel[:, 60:72]
            tB = sel[:, 72:84]
            kp_dve = [0, 0]
            mp_dve = 0
            for bpos in range(4):
                p.op("vector", "tensor_tensor_scan", out=rs_sb[:],
                     data0=RT1[:, bpos * 512:bpos * 512 + 512],
                     data1=RT1[:, bpos * 512:bpos * 512 + 512],
                     initial=0.0, op0=ALU.add, op1=ALU.max,
                     waits=[("dve", thr_dve)])
                p.op("vector", "tensor_scalar", rs_sb[:], rs_sb[:],
                     thr_sb[:, 2 * bpos:2 * bpos + 1], None, ALU.is_lt)
                p.op("vector", "tensor_tensor_scan", out=rs2_sb[:],
                     data0=RT2[:, bpos * 512:bpos * 512 + 512],
                     data1=RT2[:, bpos * 512:bpos * 512 + 512],
                     initial=0.0, op0=ALU.add, op1=ALU.max)
                kd = p.op("vector", "tensor_scalar", rs2_sb[:], rs2_sb[:],
                          thr_sb[:, 2 * bpos + 1:2 * bpos + 2], None,
                          ALU.is_lt, inc="dve")
                for q in range(4):
                    tt = bpos * 4 + q
                    tk1 = p.op("tensor", "transpose", kp1,
                               rs_sb[:, q * 128:q * 128 + 128],
                               id_sb[0:12, 0:12],
                               waits=[("dve", kd), ("dve", kp_dve[0])],
                               inc="pe")
                    tk2 = p.op("tensor", "transpose", kp2,
                               rs2_sb[:, q * 128:q * 128 + 128],
                               id_sb[0:12, 0:12],
                               waits=[("dve", kp_dve[1])], inc="pe")
                    p.op("vector", "tensor_copy", ktm1, kp1,
                         waits=[("pe", tk1)])
                    kp_dve[0] = p.op("vector", "tensor_copy", ktm2, kp2,
                                     waits=[("pe", tk2)], inc="dve")
                    kp_dve[1] = kp_dve[0]
                    p.op("vector", "tensor_tensor", tA, ktm1,
                         OS1[:, tt * 12:tt * 12 + 12], ALU.mult)
                    p.op("vector", "tensor_scalar", tA, tA,
                         RWS[:, 2 * tt:2 * tt + 1], None, ALU.mult)
                    p.op("vector", "tensor_tensor", tB, ktm2,
                         OS2[:, tt * 12:tt * 12 + 12], ALU.mult)
                    p.op("vector", "tensor_scalar", tB, tB,
                         RWS[:, 2 * tt + 1:2 * tt + 2], None, ALU.mult)
                    md = p.op("vector", "tensor_tensor", tA, tA, tB, ALU.add,
                              inc="dve")
                    tm = p.op("tensor", "transpose", mp, tA, id_sb[:],
                              waits=[("dve", md), ("dve", mp_dve)], inc="pe")
                    mp_dve = p.op("vector", "tensor_copy",
                                  M[:, tt * 128:tt * 128 + 128], mp,
                                  waits=[("pe", tm)], inc="dve")
            m_dve[0] = mp_dve

        w13s = [pb(0, 4096), pb(4096, 4096), pb(8192, 4096), pb(12288, 4096)]
        w13_n = [0, 0, 0, 0]
        w13_rd = [0, 0, 0, 0]
        s1 = pb(16384, 1024)
        g1 = pb(17408, 1024)
        mbf = pf(2048, 1024)
        w2sl = [pb(18432, 3072), pb(21504, 3072)]
        w2_nl = [0, 0]
        w2_rdl = [0, 0]
        h1 = psA[:, 0:1024]
        h3 = psA[:, 1024:2048]
        mb = psA[:, 2048:3072]
        mrow_ps = psA[0:1, 3072:4096]
        h_free = 0
        mb_free = 0
        mrow_rd = 0
        gs_rd = 0
        py_free = [0, 0]
        first_h = True
        for g in range(2):
            for e in range(NE):
                sA = 2 * (e % 2)
                w = [("pe", w13_rd[sA])]
                if first_h:
                    w += [("dve", xln_dve), ("dve", sel_done)]
                w13_n[sA] = p.dma("gpsimd", "w130" if sA == 0 else "w131",
                                  pb(sA * 4096, 8192),
                                  w13[e], waits=w + [("pe", w13_rd[sA + 1])])
                w13_n[sA + 1] = w13_n[sA]
                for fc in range(2):
                    for half in range(2):
                        for c in range(DC):
                            w = []
                            if half == 0 and c == 0:
                                w = [("act", h_free)]
                                if fc == 0:
                                    w += [("w130" if sA == 0 else "w131",
                                           w13_n[sA]),
                                          ("dve", xln_dve)]
                                    if first_h:
                                        w += [("dve", sel_done)]
                                    prev = state.get(("gsrd", e), 0)
                                    if prev:
                                        w.append(("pe", prev))
                            p.op("tensor", "matmul",
                                 h1[:, half * 512:half * 512 + 512],
                                 w13s[sA][:, (fc * DC + c) * 128:
                                          (fc * DC + c) * 128 + 128],
                                 xln[:, c * 2048 + g * 1024 + half * 512:
                                     c * 2048 + g * 1024 + half * 512 + 512],
                                 start=(c == 0), stop=(c == DC - 1),
                                 skip_group_check=True, waits=w)
                    for half in range(2):
                        for c in range(DC):
                            hpe = p.op(
                                "tensor", "matmul",
                                h3[:, half * 512:half * 512 + 512],
                                w13s[sA + 1][:, (fc * DC + c) * 128:
                                             (fc * DC + c) * 128 + 128],
                                xln[:, c * 2048 + g * 1024 + half * 512:
                                    c * 2048 + g * 1024 + half * 512 + 512],
                                start=(c == 0), stop=(c == DC - 1),
                                skip_group_check=True,
                                inc="pe" if (half == 1 and c == DC - 1)
                                else None)
                    if fc == 1:
                        w13_rd[sA] = hpe
                        w13_rd[sA + 1] = hpe
                    ev1 = p.op("scalar", "copy", s1, h1,
                               waits=[("pe", hpe), ("dve", gs_rd)],
                               inc="act")
                    p1 = p.op("vector", "tensor_tensor", g1, s1, h3,
                              ALU.mult, waits=[("act", ev1)], inc="dve")
                    sact = p.op("scalar", "activation", s1, g1, AF.Silu,
                                waits=[("dve", p1)], inc="act")
                    gd = p.op("vector", "tensor_copy",
                              GS[:, (e * 2 + fc) * 1024:
                                 (e * 2 + fc) * 1024 + 1024], s1,
                              waits=[("act", sact)], inc="dve")
                    gs_rd = gd
                    h_free = p.op("scalar", "copy", rw_sb[:, 6:7],
                                  cb_sb[:, 3:4], waits=[("dve", gd)],
                                  inc="act")
                first_h = False
            h_all = p.now("dve")
            if g == 0:
                emit_mphase()
            gate_d = p.now("dve")
            gate_x = p.full("xw0", "xw1")
            for e in range(NE):
                for jj in range(2):
                    w = []
                    if jj == 0:
                        w = [("dve", m_dve[0]), ("dve", mrow_rd),
                             ("dve", gate_d)]
                    mpe = p.op("tensor", "matmul",
                               mrow_ps[0:1, jj * 512:jj * 512 + 512],
                               oh_sb[:, e:e + 1],
                               M[:, g * 1024 + jj * 512:
                                 g * 1024 + jj * 512 + 512],
                               start=True, stop=True, skip_group_check=True,
                               waits=w)
                mrow_rd = p.op("vector", "tensor_copy", ROWS[0:1, 1024:2048],
                               mrow_ps, waits=[("pe", mpe)], inc="dve")
                for jj in range(2):
                    w = [("dve", mrow_rd), ("dve", mb_free)] if jj == 0 else []
                    mbpe = p.op(
                        "tensor", "matmul", mb[:, jj * 512:jj * 512 + 512],
                        onesr[:],
                        ROWS[0:1, 1024 + jj * 512:1024 + jj * 512 + 512],
                        start=True, stop=True, skip_group_check=True,
                        waits=w, inc="pe" if jj == 1 else None)
                mbd = p.op("vector", "tensor_copy", mbf, mb,
                           waits=[("pe", mbpe)] + gate_x, inc="dve")
                mb_free = mbd
                for fc in range(2):
                    sc = p.op("vector", "tensor_tensor",
                              GS[:, (e * 2 + fc) * 1024:
                                 (e * 2 + fc) * 1024 + 1024],
                              GS[:, (e * 2 + fc) * 1024:
                                 (e * 2 + fc) * 1024 + 1024], mbf,
                              ALU.mult, waits=[("dve", mbd)], inc="dve")
            gs_scaled = p.now("dve")
            for f in range(16):
                iw = f % 2
                w = [("pe", w2_rdl[iw])]
                if f == 0:
                    w.append(("dve", gs_scaled))
                w2_nl[iw] = p.dma("gpsimd", "w132" if iw == 0 else "w133",
                                  w2sl[iw], w2p[f], waits=w)
                if True:
                    for blk in range(2):
                        tb = g * 2 + blk
                        sl = blk % 2
                        py = psA[:, 3072 + sl * 512:3072 + sl * 512 + 512]
                        for ef in range(2 * NE):
                            e, fc = ef // 2, ef % 2
                            w = []
                            if ef == 0:
                                w = [("w132" if iw == 0 else "w133",
                                      w2_nl[iw]), ("dve", py_free[sl]),
                                     ("dve", mrow_rd),
                                     ("dve", gs_scaled)]
                            ype = p.op(
                                "tensor", "matmul", py,
                                w2sl[iw][:, ef * 128:ef * 128 + 128],
                                GS[:, ef * 1024 + blk * 512:
                                   ef * 1024 + blk * 512 + 512],
                                start=(ef == 0),
                                stop=(ef == 2 * NE - 1),
                                skip_group_check=True, waits=w,
                                inc="pe" if ef == 2 * NE - 1
                                else None)
                        i2 = (f * 2 + blk) % 2
                        xof = pf(2048 + i2 * 512, 512)
                        xw = [(f"xw{i2}", state[("xw", i2)])]
                        if f == 0 and blk == 0:
                            xw.append(("dve", gs_scaled))
                            if g == 0:
                                xw += gate
                        xq = p.dma("sync", f"xl{i2}", xof,
                                   xsrc[f * 128:(f + 1) * 128,
                                        tb * 512:tb * 512 + 512], waits=xw)
                        ad = p.op("vector", "tensor_tensor", xof, xof, py,
                                  ALU.add, waits=[("pe", ype),
                                                  (f"xl{i2}", xq)],
                                  inc="dve")
                        py_free[sl] = ad
                        state[("xw", i2)] = p.dma(
                            "sync", f"xw{i2}",
                            xdst[f * 128:(f + 1) * 128,
                                 tb * 512:tb * 512 + 512], xof,
                            waits=[("dve", ad)])
                    w2_rdl[iw] = p.now("pe")
            for e in range(NE):
                state[("gsrd", e)] = p.now("pe")
        return p.full("xw0", "xw1")

    def emit_attn(layer, xsrc, xdst, gate):
        xd = emit_ln(xsrc, gate)
        wsl_rd[0] = wsl_rd[1] = p.now("pe")
        cs_src = CSa if layer == 0 else CSb
        CS = pb(16384, 4096)
        p.dma("gpsimd", "cs", CS[:, 0:2048], cs_src[0], waits=gate)
        p.dma("gpsimd", "cs", CS[:, 2048:4096], cs_src[1])
        RP["cs_n"] = p.now("cs")
        RP["pr_w"] = []
        RP["tq_w"] = [[], []]
        RP["qo_w"] = [[], []]
        wk_src = wak if layer == 0 else wbk
        wq_src = waq if layer == 0 else wbq
        wv_src = wav if layer == 0 else wbv
        wo_src = wpa if layer == 0 else wbo

        def kdst(j):
            def f(blk):
                return kown.ap()[j * 128:(j + 1) * 128,
                                 blk * 512:blk * 512 + 512]
            return f

        def qdst(h):
            def f(blk):
                return Qd[h * 128:(h + 1) * 128, blk * 512:blk * 512 + 512]
            return f

        for j in range(NKV):
            proj_head(j, wk_src[j], xd, rope_epilog(layer, kdst(j), "kw"))
        kq = p.full("kw0", "kw1")
        vq = emit_vproj(wv_src, xd)
        ck, cv = ("cka", "cva") if layer == 0 else ("ckb", "cvb")
        p.op("gpsimd", "collective_compute", "AllGather", ALU.bypass,
             replica_groups=[[0, 1], [2, 3], [4, 5], [6, 7]],
             ins=[kown.ap().opt()], outs=[kg.ap().opt()],
             waits=kq + vq, inc=ck)
        p.op("gpsimd", "collective_compute", "AllGather", ALU.bypass,
             replica_groups=[[0, 1], [2, 3], [4, 5], [6, 7]],
             ins=[vown.ap().opt()], outs=[vg.ap().opt()], inc=cv)
        for h in range(NH):
            proj_head(h, wq_src[h], xd, rope_epilog(layer, qdst(h), "qw"))
        state["qd_done"] = p.full("qw0", "qw1")
        emit_core(layer)
        if layer == 0:
            emit_wo(0, wo_src, None, None, gate)
            state["yd_done"] = p.full("yw0", "yw1")
            return emit_rms(xsrc, xdst, gate)
        else:
            return emit_wo(1, wo_src, xsrc, xdst, gate)

    end = emit_attn(0, x_fm, Xd, CONST_S + CONST_G)
    xd = emit_ln(Xd, end)
    end = emit_ffn(xd, 0, Xd, Xd, end)
    end = emit_attn(1, Xd, Xd, end)
    xd = emit_ln(Xd, end)
    end = emit_ffn(xd, 1, Xd, y_fm, end)
    p.op("sync", "nop", waits=[(s, p.now(s)) for s in SEMS])

    sems = {}
    with ExitStack() as sst:
        for s in SEMS:
            sems[s] = sst.enter_context(nc.semaphore(s))
        with nc.Block() as block:
            block.sync(_replay(p, sems, "sync"))
            block.gpsimd(_replay(p, sems, "gpsimd"))
            block.tensor(_replay(p, sems, "tensor"))
            block.scalar(_replay(p, sems, "scalar"))
            block.vector(_replay(p, sems, "vector"))
    st_ctx.close()
    return nc


def _perm_a():
    pm = np.zeros(128, np.int64)
    pm[0:64] = np.arange(0, 128, 2)
    pm[64:128] = np.arange(1, 128, 2)
    return pm


def _perm_b():
    pm = np.zeros(128, np.int64)
    pm[0:32] = np.arange(0, 64, 2)
    pm[32:64] = np.arange(1, 64, 2)
    pm[64:128] = np.arange(64, 128)
    return pm


def _slab_pack(w):
    Dd, Ff = w.shape
    return np.ascontiguousarray(
        w.reshape(Dd // 128, 128, Ff // 128, 128).transpose(2, 1, 0, 3)
        .reshape(Ff // 128, 128, Dd))


def _vslab_pack(wv):
    out = np.zeros((2, 128, 4096), np.float32)
    for u in range(2):
        for c in range(16):
            for jj in range(2):
                j = 2 * u + jj
                out[u, :, c * 256 + jj * 128:c * 256 + jj * 128 + 128] = \
                    wv[c * 128:(c + 1) * 128, j * 128:(j + 1) * 128]
    return np.ascontiguousarray(out)


def _my_tokens(cls):
    idx = []
    for b in BLOCKS[cls]:
        idx.extend(range(b * 512, (b + 1) * 512))
    return np.array(idx, np.int64)


def _rope_tables(kind, pos):
    pos = pos.astype(np.float64)
    C = np.zeros((128, NT))
    S = np.zeros((128, NT))
    if kind == 0:
        inv = 10000.0 ** (-np.arange(0, 128, 2, dtype=np.float64) / 128.0)
        ang = inv[:, None] * pos[None, :]
        C[0:64] = np.cos(ang)
        C[64:128] = np.cos(ang)
        S[0:64] = -np.sin(ang)
        S[64:128] = np.sin(ang)
    else:
        inv = 10000.0 ** (-np.arange(0, 64, 2, dtype=np.float64) / 64.0)
        ang = inv[:, None] * pos[None, :]
        C[0:32] = np.cos(ang)
        C[32:64] = np.cos(ang)
        C[64:128] = 1.0
        S[0:32] = -np.sin(ang)
        S[32:64] = np.sin(ang)
    return C.astype(np.float32), S.astype(np.float32)


def _prm_a_mat():
    m = np.zeros((128, 128), np.float32)
    for i in range(64):
        m[64 + i, i] = 1.0
        m[i, 64 + i] = 1.0
    return m


def _prm_b_mat():
    m = np.zeros((128, 128), np.float32)
    for i in range(32):
        m[32 + i, i] = 1.0
        m[i, 32 + i] = 1.0
    return m


def _prec_fill(core):
    def key(r, pp, s):
        return (s, r // 2, BLOCKS[r % 2][pp])
    out = np.zeros((64, 8), np.float32)
    for r in range(8):
        for u in range(8):
            pp, s = u // 2, u % 2
            ku = key(r, pp, s)
            for mu in range(8):
                mp, ms = mu // 2, mu % 2
                if ku < key(core, mp, ms):
                    out[r * 8 + u, mu] = 1.0
    return out


def kernel(L0_attn_a_x, L0_attn_a_w_attn, L0_attn_a_w_proj, L0_attn_a_norm_w,
           L0_attn_b_Wq, L0_attn_b_Wk, L0_attn_b_Wv, L0_attn_b_Wo,
           L0_ffn_gate_w, L0_ffn_w1, L0_ffn_w2, L0_ffn_w3):
    x = np.asarray(L0_attn_a_x, np.float32)
    wa = np.asarray(L0_attn_a_w_attn, np.float32)
    wpa_ = np.asarray(L0_attn_a_w_proj, np.float32)
    nw = np.asarray(L0_attn_a_norm_w, np.float32)
    wq = np.asarray(L0_attn_b_Wq, np.float32)
    wk = np.asarray(L0_attn_b_Wk, np.float32)
    wv = np.asarray(L0_attn_b_Wv, np.float32)
    wo = np.asarray(L0_attn_b_Wo, np.float32)
    gwt = np.asarray(L0_ffn_gate_w, np.float32)
    w1 = np.asarray(L0_ffn_w1, np.float32)
    w2 = np.asarray(L0_ffn_w2, np.float32)
    w3 = np.asarray(L0_ffn_w3, np.float32)

    pa, pbm = _perm_a(), _perm_b()
    waq_full = wa[:, 0:2048].reshape(D, NH, 128)[:, :, pa].reshape(D, 2048)
    wak_full = wa[:, 2048:2560].reshape(D, NKV, 128)[:, :, pa].reshape(D, 512)
    wbq_full = wq.reshape(D, NH, 128)[:, :, pbm].reshape(D, 2048)
    wbk_full = wk.reshape(D, NKV, 128)[:, :, pbm].reshape(D, 512)

    shared = dict(
        waq=_slab_pack(waq_full),
        wak=_slab_pack(wak_full),
        wav=_vslab_pack(wa[:, 2560:3072]),
        wpa=_slab_pack(wpa_),
        nwa=np.ascontiguousarray(nw.reshape(DC, 128).T),
        wbq=_slab_pack(wbq_full),
        wbk=_slab_pack(wbk_full),
        wbv=_vslab_pack(wv),
        wbo=_slab_pack(wo),
        gw=np.ascontiguousarray(
            gwt.T.reshape(DC, 128, NE).transpose(1, 0, 2).reshape(
                128, DC * NE)),
        idn=np.eye(128, dtype=np.float32),
        onescol=np.ones((128, 1), np.float32),
        onesrow=np.ones((1, 128), np.float32),
        oh12=np.eye(12, dtype=np.float32),
        cbias=np.ascontiguousarray(np.tile(
            np.array([EXPB, 1e-5, 1e-6, 0.0], np.float32), (128, 1))),
        prm=np.stack([_prm_a_mat(), _prm_b_mat()]),
    )
    w13p = np.zeros((NE, 128, 8192), np.float32)
    for e in range(NE):
        for k_, wsrc in ((0, w1), (1, w3)):
            wt = wsrc[e].T
            s = wt.reshape(DC, 128, 2, 128)
            w13p[e, :, k_ * 4096:(k_ + 1) * 4096] = \
                s.transpose(1, 2, 0, 3).reshape(128, 4096)
    shared["w13"] = np.ascontiguousarray(w13p)
    w2pp = np.zeros((16, 128, 3072), np.float32)
    for e in range(NE):
        wt = w2[e].T
        for f in range(16):
            for fc in range(2):
                w2pp[f, :, (e * 2 + fc) * 128:(e * 2 + fc) * 128 + 128] = \
                    wt[fc * 128:(fc + 1) * 128, f * 128:(f + 1) * 128]
    shared["w2p"] = np.ascontiguousarray(w2pp)

    korder = np.concatenate([_my_tokens(0), _my_tokens(1)])
    kidc = np.ascontiguousarray(korder.reshape(32, 128).T.astype(np.float32))

    nc = build_nc()
    in_maps = []
    for core in range(8):
        b, cls = core // 2, core % 2
        toks = _my_tokens(cls)
        pos = toks.astype(np.float32)
        Ca, Sa = _rope_tables(0, pos)
        Cb, Sb = _rope_tables(1, pos)
        m = dict(shared)
        m["x_fm"] = np.ascontiguousarray(x[b][toks].T)
        m["csa"] = np.ascontiguousarray(np.stack([Ca, Sa]))
        m["csb"] = np.ascontiguousarray(np.stack([Cb, Sb]))
        m["qidc"] = np.ascontiguousarray(np.tile(pos[None, :], (128, 1)))
        m["kidc"] = kidc
        m["prec"] = _prec_fill(core)
        in_maps.append(m)

    res = run_bass_kernel_spmd(nc, in_maps, core_ids=list(range(8)))
    out = np.empty((4, 4096, D), np.float32)
    for core in range(8):
        b, cls = core // 2, core % 2
        out[b][_my_tokens(cls)] = res.results[core]["y_fm"].T
    return out


if __name__ == "__main__":
    print("building program...")
    nc = build_nc()
    print("instructions:", nc.n_instructions())
    print("sbuf bytes remaining:", nc.sbuf_bytes_remaining)
